# version 33
# baseline (speedup 1.0000x reference)
import numpy as np
import ml_dtypes
import concourse.bass as bass
import concourse.mybir as mybir
from concourse.bass_utils import run_bass_kernel_spmd

F32 = mybir.dt.float32
BF16 = mybir.dt.bfloat16
AF = mybir.ActivationFunctionType
ALU = mybir.AluOpType
AX = mybir.AxisListType

S = 2048
DM = 2048
H = 8
HD = 128
NSP = 8
FH = 2816
FC = 22
KC = 16
NGRP = 3
GRPS = [8, 7, 7]
FCG = 8
TSTR = 256
NSTR = S // TSTR
ISQ = float(1.0 / np.sqrt(np.float32(128.0)))
SQH = float(np.sqrt(np.float32(128.0)))
NW_ELEM = 5632 * 2048

ENGS = ("sync", "vector", "tensor", "scalar", "gpsimd")
PAIRS = [[0, 1], [2, 3], [4, 5], [6, 7]]
ALL8 = [[0, 1, 2, 3, 4, 5, 6, 7]]


class Sem:
    __slots__ = ("h", "n", "name")

    def __init__(self, h, name):
        self.h = h
        self.n = 0
        self.name = name


class B:
    def __init__(self):
        self.ops = {e: [] for e in ENGS}
        self.waited = {}

    def emit(self, eng, fn, waits=(), inc=None, amt=1):
        rw = []
        for wv in waits:
            if wv is None:
                continue
            (s, v) = wv
            if v is None or v <= 0:
                continue
            key = (eng, s.name)
            if self.waited.get(key, -1) >= v:
                continue
            self.waited[key] = v
            rw.append((s, v))
        if inc is not None:
            inc.n += amt
        tgt = inc.n if inc is not None else None

        def run(e, fn=fn, rw=tuple(rw), inc=inc, amt=amt):
            for (s, v) in rw:
                e.wait_ge(s.h, v)
            r = fn(e)
            if inc is not None:
                r.then_inc(inc.h, amt)
        self.ops[eng].append(run)
        return tgt

    def flush(self, block):
        ops = self.ops
        self.ops = {e: [] for e in ENGS}

        @block.sync
        def _(e):
            for f in ops["sync"]:
                f(e)

        @block.vector
        def _(e):
            for f in ops["vector"]:
                f(e)

        @block.tensor
        def _(e):
            for f in ops["tensor"]:
                f(e)

        @block.scalar
        def _(e):
            for f in ops["scalar"]:
                f(e)

        @block.gpsimd
        def _(e):
            for f in ops["gpsimd"]:
                f(e)


def build(debug=False, safe=0):
    nc = bass.Bass("TRN2", target_bir_lowering=False)
    dt = nc.dram_tensor

    xTb = dt("xTb", [128, KC * S], F32, kind="ExternalInput")
    xOwn = dt("xOwn", [1024, DM], F32, kind="ExternalInput")
    wq = dt("wq", [H, 128, KC * 128], F32, kind="ExternalInput")
    wk = dt("wk", [H, 128, KC * 128], F32, kind="ExternalInput")
    wv = dt("wv", [H, 128, KC * 128], F32, kind="ExternalInput")
    wo = dt("wo", [H, 128, DM], F32, kind="ExternalInput")
    w1 = dt("w1", [FC, 128, KC * 128], F32, kind="ExternalInput")
    w2 = dt("w2", [FC, 128, KC * 128], F32, kind="ExternalInput")
    w3 = dt("w3", [128, 45056], F32, kind="ExternalInput")
    pos = dt("pos", [NSP, H * HD], F32, kind="ExternalInput")
    dirs = dt("dirs", [NSP, H * HD], F32, kind="ExternalInput")
    lsT = dt("lsT", [NSP, H], F32, kind="ExternalInput")
    laT = dt("laT", [NSP, H], F32, kind="ExternalInput")
    scal = dt("scal", [1, 4], F32, kind="ExternalInput")

    out = dt("out", [1024, DM], F32, kind="ExternalOutput")
    if debug:
        dbg_xn = dt("dbg_xn", [128, 2048], BF16, kind="ExternalOutput")
        dbg_q = dt("dbg_q", [128, S], BF16, kind="ExternalOutput")
        dbg_k = dt("dbg_k", [128, S], BF16, kind="ExternalOutput")
        dbg_v = dt("dbg_v", [128, 16 * 129], BF16, kind="ExternalOutput")
        dbg_aq = dt("dbg_aq", [16, S], BF16, kind="ExternalOutput")
        dbg_bk = dt("dbg_bk", [16, S], BF16, kind="ExternalOutput")
        dbg_qsq = dt("dbg_qsq", [1, S], BF16, kind="ExternalOutput")
        dbg_ot = dt("dbg_ot", [128, H * S], BF16, kind="ExternalOutput")
        dbg_expb = dt("dbg_expb", [128, 1024], BF16, kind="ExternalOutput")
        dbg_gb = dt("dbg_gb", [128, 8], F32, kind="ExternalOutput")
        dbg_ostd = dt("dbg_ostd", [128, 512], F32, kind="ExternalOutput")
        dbg_rows = dt("dbg_rows", [1, 16], F32, kind="ExternalOutput")
        dbg_ars = dt("dbg_ars", [1024, DM], F32, kind="ExternalOutput")
        dbg_hid = dt("dbg_hid", [128, FCG * S], BF16, kind="ExternalOutput")
        dbg_w3q = dt("dbg_w3q", [128, FCG * 512], BF16, kind="ExternalOutput")
        dbg_frs = dt("dbg_frs", [1024, NGRP * 1024], F32, kind="ExternalOutput")
        dbg_pdt = dt("dbg_pdt", [128, H * 16], BF16, kind="ExternalOutput")
        dbg_cols = dt("dbg_cols", [128, 16], F32, kind="ExternalOutput")
        dbg_abr = dt("dbg_abr", [1, 2048], F32, kind="ExternalOutput")
        dbg_fp = dt("dbg_fp", [S, 1024], F32, kind="ExternalOutput")
        dbg_xs = dt("dbg_xs", [128, 4096], F32, kind="ExternalOutput")
        dbg_abc = dt("dbg_abc", [128, 1024], F32, kind="ExternalOutput")

    part_secs = [dt(f"part_sec{j}", [S, DM], F32) for j in range(1 + NGRP)]
    rs_secs = [dt(f"rs_sec{j}", [1024, DM], F32) for j in range(1 + NGRP)]
    acc_sec = dt("acc_sec", [1024, DM], F32)
    gam_in = dt("gam_in", [1, 3], F32)
    gam_out = dt("gam_out", [1, 3], F32)

    ident_h = nc.inline_tensor(np.eye(128, dtype=ml_dtypes.bfloat16), "ident")

    from contextlib import ExitStack
    es0 = ExitStack()
    E = es0.enter_context

    sQ = Sem(E(nc.semaphore("sQ")), "sQ")
    sG = Sem(E(nc.semaphore("sG")), "sG")
    sC = Sem(E(nc.semaphore("sC")), "sC")
    sP = Sem(E(nc.semaphore("sP")), "sP")
    sS = Sem(E(nc.semaphore("sS")), "sS")
    sV = Sem(E(nc.semaphore("sV")), "sV")
    sL = Sem(E(nc.semaphore("sL")), "sL")
    sO = Sem(E(nc.semaphore("sO")), "sO")
    sSD = Sem(E(nc.semaphore("sSD")), "sSD")
    sX = [Sem(E(nc.semaphore(f"sX{i}")), f"sX{i}") for i in range(2)]
    sGU = [Sem(E(nc.semaphore(f"sGU{i}")), f"sGU{i}") for i in range(2)]
    sW = Sem(E(nc.semaphore("sW")), "sW")
    sW1 = Sem(E(nc.semaphore("sW1")), "sW1")
    sW2 = Sem(E(nc.semaphore("sW2")), "sW2")
    sW3 = [Sem(E(nc.semaphore(f"sW3{i}")), f"sW3{i}") for i in range(2)]
    sF = [Sem(E(nc.semaphore(f"sF{i}")), f"sF{i}") for i in range(2)]
    sA = [Sem(E(nc.semaphore(f"sA{i}")), f"sA{i}") for i in range(2)]
    ALLS0 = [sQ, sG, sP, sS, sV, sL, sSD, sW, sW1, sW2] + sX + sGU + sW3 + sF + sA

    xnT = E(nc.sbuf_tensor("xnT", [128, KC * S], BF16))
    id_sb = E(nc.sbuf_tensor("id_sb", [128, 128], BF16))
    onesf = E(nc.sbuf_tensor("onesf", [1, 128], F32))
    onescol = E(nc.sbuf_tensor("onescol", [128, 1], BF16))
    c8 = E(nc.sbuf_tensor("c8", [1, NSP], BF16))
    ones8 = E(nc.sbuf_tensor("ones8", [NSP, 1], F32))
    lsT_sb = E(nc.sbuf_tensor("lsT_sb", [NSP, H], F32))
    laT_sb = E(nc.sbuf_tensor("laT_sb", [NSP, H], F32))
    scal_sb = E(nc.sbuf_tensor("scal_sb", [1, 4], F32))
    rs2_all = E(nc.sbuf_tensor("rs2_all", [NSP, H], F32))
    bq_all = E(nc.sbuf_tensor("bq_all", [NSP, H], F32))
    bk_all = E(nc.sbuf_tensor("bk_all", [NSP, H], F32))
    bt_all = E(nc.sbuf_tensor("bt_all", [NSP, H], F32))
    pdT_all = E(nc.sbuf_tensor("pdT_all", [128, H * 16], BF16))
    c0col = E(nc.sbuf_tensor("c0col", [NSP, 1], F32))
    c1col = E(nc.sbuf_tensor("c1col", [NSP, 1], F32))
    blcol = E(nc.sbuf_tensor("blcol", [128, 1], F32))
    ombcol = E(nc.sbuf_tensor("ombcol", [128, 1], F32))
    sm01 = E(nc.sbuf_tensor("sm01", [1, 4], F32))
    gacc = E(nc.sbuf_tensor("gacc", [128, 136], F32))
    gcol3 = E(nc.sbuf_tensor("gcol3", [128, 4], F32))
    grow3 = E(nc.sbuf_tensor("grow3", [1, 4], F32))

    bld = B()
    em = bld.emit

    def DMA(eng, dst, src, sem=sQ, waits=()):
        return em(eng, lambda e, d=dst, s=src: e.dma_start(out=d, in_=s),
                  waits=waits, inc=sem, amt=16)

    ALLS = ALLS0

    def allw():
        return [(s, s.n) for s in ALLS]

    def vch(fn, waits=()):
        return em("vector", fn, waits=list(waits) + [(sV, sV.n)], inc=sV)

    gamma_state = {"j": 0, "sc": []}
    NGU = 6 * FC

    def emit_gamma_units(n):
        for _ in range(n):
            j = gamma_state["j"]
            if j >= NGU:
                return
            gamma_state["j"] = j + 1
            m, blk2 = j // (2 * FC), j % (2 * FC)
            blk, half = blk2 // 2, blk2 % 2
            if m == 0:
                src = w1[blk, :, half * 1024:(half + 1) * 1024]
            elif m == 1:
                src = w2[blk, :, half * 1024:(half + 1) * 1024]
            else:
                cc = (2 * blk + half) * 1024
                src = w3[:, cc:cc + 1024]
            buf = j % 2
            wts = [(sS, gamma_state["sc"][j - 2])] if j >= 2 else []
            d = DMA("sync", gstage[:, buf * 1024:(buf + 1) * 1024], src, sem=sGU[buf], waits=wts)
            t = em("scalar",
                   lambda e, buf=buf, j=j: e.activation(gdump[:, :], gstage[:, buf * 1024:(buf + 1) * 1024],
                                                        AF.Abs, accum_out=gacc[:, j:j + 1]),
                   waits=[(sGU[buf], d), (sS, sS.n)], inc=sS)
            gamma_state["sc"].append(t)

    esA = ExitStack()
    EA = esA.enter_context
    oT = EA(nc.sbuf_tensor("oT", [128, H * S], BF16))
    gdump = EA(nc.sbuf_tensor("gdump", [128, 1024], BF16))
    gstage = EA(nc.sbuf_tensor("gstage", [128, 2 * 1024], F32))

    DMA("sync", lsT_sb[:, :], lsT[:, :])
    DMA("sync", laT_sb[:, :], laT[:, :])
    DMA("sync", scal_sb[:, :], scal[:, :])
    DMA("sync", id_sb[:, :], ident_h[:, :])

    em("vector", lambda e: e.memset(onesf[:], 1.0))
    em("vector", lambda e: e.memset(onescol[:], 1.0))
    em("vector", lambda e: e.memset(ones8[:], 1.0))
    em("vector", lambda e: e.memset(c8[:], -0.5), inc=sV)

    with nc.sbuf_tensor("pos_sb", [NSP, H * HD], F32) as pos_sb, \
         nc.sbuf_tensor("dirs_sb", [NSP, H * HD], F32) as dirs_sb, \
         nc.sbuf_tensor("dn_sb", [NSP, H * HD], F32) as dn_sb, \
         nc.sbuf_tensor("sc8a", [NSP, 4], F32) as sc8a, \
         nc.sbuf_tensor("sc8b", [NSP, 128], F32) as sc8b, \
         nc.sbuf_tensor("pbf", [NSP, 128], BF16) as pbf, \
         nc.sbuf_tensor("dnbf", [NSP, 128], BF16) as dnbf, \
         nc.psum_tensor("ps_t8", [128, NSP], BF16) as ps_t8, \
         nc.psum_tensor("ps_c", [128, 1], F32) as ps_c:
        DMA("sync", pos_sb[:, :], pos[:, :])
        w_in0 = DMA("sync", dirs_sb[:, :], dirs[:, :])
        for h in range(H):
            hs = slice(h * HD, (h + 1) * HD)
            vch(lambda e, hs=hs: e.tensor_mul(sc8b[:, :], dirs_sb[0:NSP, hs], dirs_sb[0:NSP, hs]),
                waits=[(sQ, w_in0)])
            vv = vch(lambda e: e.tensor_reduce(sc8a[:, 0:1], sc8b[:, :], AX.X, ALU.add))
            t = em("scalar", lambda e: e.activation(sc8a[:, 1:2], sc8a[:, 0:1], AF.Sqrt),
                   waits=((sV, vv),), inc=sS)
            vch(lambda e: e.tensor_scalar_add(sc8a[:, 1:2], sc8a[:, 1:2], 1e-8), waits=[(sS, t)])
            vch(lambda e: e.reciprocal(sc8a[:, 2:3], sc8a[:, 1:2]))
            vch(lambda e, hs=hs: e.tensor_single_scalar(dn_sb[0:NSP, hs], dirs_sb[0:NSP, hs],
                                                        sc8a[0:NSP, 2:3], ALU.mult))
            vch(lambda e, hs=hs: e.tensor_mul(sc8b[:, :], pos_sb[0:NSP, hs], pos_sb[0:NSP, hs]))
            vch(lambda e: e.tensor_reduce(sc8a[:, 0:1], sc8b[:, :], AX.X, ALU.add))
            vch(lambda e, hs=hs: e.tensor_mul(sc8b[:, :], pos_sb[0:NSP, hs], dn_sb[0:NSP, hs]))
            vv = vch(lambda e: e.tensor_reduce(sc8a[:, 1:2], sc8b[:, :], AX.X, ALU.add))
            t = em("scalar", lambda e, h=h: e.activation(rs2_all[0:NSP, h:h + 1], lsT_sb[0:NSP, h:h + 1],
                                                         AF.Exp, bias=0.0, scale=-2.0),
                   waits=((sQ, w_in0),), inc=sS)
            vch(lambda e, h=h: e.tensor_mul(sc8a[:, 3:4], sc8a[:, 0:1], rs2_all[0:NSP, h:h + 1]),
                waits=[(sS, t)])
            vch(lambda e, h=h: e.tensor_scalar_mul(bq_all[0:NSP, h:h + 1], sc8a[:, 3:4], -0.5))
            vch(lambda e, h=h: e.tensor_add(bk_all[0:NSP, h:h + 1], bq_all[0:NSP, h:h + 1],
                                            laT_sb[0:NSP, h:h + 1]))
            vch(lambda e, h=h: e.tensor_scalar_mul(bt_all[0:NSP, h:h + 1], sc8a[:, 1:2], -1.0))
            vch(lambda e, hs=hs: e.tensor_copy(pbf[:], pos_sb[0:NSP, hs]))
            vv = vch(lambda e, hs=hs: e.tensor_copy(dnbf[:], dn_sb[0:NSP, hs]))
            t = em("tensor", lambda e: e.transpose(ps_t8[:], pbf[:], id_sb[0:NSP, 0:NSP]),
                   waits=((sV, vv),), inc=sP)
            vv = em("vector", lambda e, h=h: e.tensor_copy(pdT_all[:, h * 16:h * 16 + 8], ps_t8[:]),
                    waits=((sP, t),), inc=sV)
            t = em("tensor", lambda e: e.transpose(ps_t8[:], dnbf[:], id_sb[0:NSP, 0:NSP]),
                   waits=((sV, vv),), inc=sP)
            em("vector", lambda e, h=h: e.tensor_copy(pdT_all[:, h * 16 + 8:h * 16 + 16], ps_t8[:]),
               waits=((sP, t),), inc=sV)

        t = em("scalar", lambda e: e.activation(sm01[0:1, 0:1], scal_sb[0:1, 2:3], AF.Sigmoid),
               waits=((sQ, w_in0),), inc=sS)
        vch(lambda e: e.tensor_scalar(sm01[0:1, 1:2], sm01[0:1, 0:1], 0.15, 0.03,
                                      ALU.mult, ALU.min), waits=[(sS, t)])
        vch(lambda e: e.tensor_scalar(sm01[0:1, 2:3], sm01[0:1, 1:2], -1.0, 1.0,
                                      ALU.mult, ALU.add))
        vv = vch(lambda e: e.tensor_scalar_mul(sm01[0:1, 3:4], scal_sb[0:1, 1:2], SQH))
        bcasts = ((sm01[0:1, 1:2], blcol, 128), (sm01[0:1, 2:3], ombcol, 128),
                  (sm01[0:1, 3:4], c0col, NSP), (scal_sb[0:1, 0:1], None, NSP))
        for (srcc, dst, prt) in bcasts:
            t = em("tensor", lambda e, srcc=srcc: e.matmul(ps_c[:], onesf[0:1, 0:128], srcc,
                                                           start=True, stop=True),
                   waits=((sV, vv),), inc=sP)
            if dst is not None:
                vv = vch(lambda e, dst=dst, prt=prt: e.tensor_copy(dst[0:prt, 0:1], ps_c[0:prt, 0:1]),
                         waits=[(sP, t)])
            else:
                vv = vch(lambda e: e.tensor_copy(gcol3[0:NSP, 3:4], ps_c[0:NSP, 0:1]),
                         waits=[(sP, t)])
        vch(lambda e: e.tensor_mul(c1col[:], c0col[:], gcol3[0:NSP, 3:4]))
    prologue_v = sV.n
    prologue_s = sS.n
    prologue_p = sP.n

    with nc.sbuf_tensor("xstage", [128, 2 * KC * TSTR], F32) as xstage, \
         nc.sbuf_tensor("xbb", [128, 2 * KC * TSTR], BF16) as xbb_t, \
         nc.sbuf_tensor("x2b", [128, 2 * KC * TSTR], BF16) as x2b_t, \
         nc.sbuf_tensor("abr", [1, 7 * TSTR], F32) as abr, \
         nc.sbuf_tensor("abc_sb", [128, 2 * 2 * TSTR], F32) as abc_sb, \
         nc.sbuf_tensor("lntmp", [128, 2 * TSTR], F32) as lntmp, \
         nc.psum_tensor([1, TSTR], F32) as ps_sx0, \
         nc.psum_tensor([1, TSTR], F32) as ps_sx1, \
         nc.psum_tensor([1, TSTR], F32) as ps_sx2, \
         nc.psum_tensor([1, TSTR], F32) as ps_sy0, \
         nc.psum_tensor([1, TSTR], F32) as ps_sy1, \
         nc.psum_tensor([1, TSTR], F32) as ps_sy2, \
         nc.psum_tensor([128, 2 * TSTR], F32) as ps_ab0, \
         nc.psum_tensor([128, 2 * TSTR], F32) as ps_ab1:
        ps_sx = [ps_sx0, ps_sx1, ps_sx2]
        ps_sy = [ps_sy0, ps_sy1, ps_sy2]
        ps_ab = [ps_ab0, ps_ab1]
        Tm, Te, Tv, Tr, Ta, Tb, Tt = [slice(i * TSTR, (i + 1) * TSTR) for i in range(7)]
        apply_done = []
        stats_read = []
        bcast_done = []
        evac_done = []
        cast_hist = []
        for s in range(NSTR):
            buf = s % 2
            b3 = s % 3
            xs = xstage[:, buf * KC * TSTR:(buf + 1) * KC * TSTR]
            xbb = xbb_t[:, buf * KC * TSTR:(buf + 1) * KC * TSTR]
            x2b = x2b_t[:, buf * KC * TSTR:(buf + 1) * KC * TSTR]
            if s >= 2:
                wts = [(sV, apply_done[s - 2])]
            else:
                wts = [(sV, prologue_v), (sS, prologue_s), (sP, prologue_p)]
            d = DMA("sync", xs, xTb[:, s * KC * TSTR:(s + 1) * KC * TSTR], sem=sX[buf], waits=wts)
            cw = [(sX[buf], d)]
            if s >= 2:
                cw.append((sP, bcast_done[s - 2]))
            if debug == 2:
                if s == 0:
                    DMA("sync", dbg_xs[:, :], xs, waits=[(sX[buf], d)])
            cast = em("vector", lambda e, xs=xs, xbb=xbb: e.tensor_copy(xbb[:], xs),
                      waits=cw, inc=sV)
            cast_hist.append(cast)
            sq = em("gpsimd", lambda e, xbb=xbb, x2b=x2b: e.tensor_mul(x2b[:], xbb[:], xbb[:]),
                    waits=[(sV, cast)], inc=sL)
            mm_w = [(sV, cast), (sL, sq), (sV, prologue_v), (sS, prologue_s)]
            if s >= 3:
                mm_w.append((sV, stats_read[s - 3]))
            for kc in range(KC):
                em("tensor", lambda e, kc=kc, b3=b3, xbb=xbb: e.matmul(
                    ps_sx[b3][:], onescol[:, 0:1], xbb[:, kc * TSTR:(kc + 1) * TSTR],
                    start=(kc == 0), stop=(kc == KC - 1)), waits=mm_w if kc == 0 else ())
            for kc in range(KC):
                em("tensor", lambda e, kc=kc, b3=b3, x2b=x2b: e.matmul(
                    ps_sy[b3][:], onescol[:, 0:1], x2b[:, kc * TSTR:(kc + 1) * TSTR],
                    start=(kc == 0), stop=(kc == KC - 1)),
                   inc=(sP if kc == KC - 1 else None))
            pmm = sP.n
            bcast_done.append(pmm)
            vch(lambda e, b3=b3: e.tensor_scalar_mul(abr[0:1, Tm], ps_sx[b3][:], 1.0 / DM),
                waits=[(sP, pmm)])
            vch(lambda e, b3=b3: e.tensor_scalar_mul(abr[0:1, Te], ps_sy[b3][:], 1.0 / DM))
            vch(lambda e: e.tensor_mul(abr[0:1, Tt], abr[0:1, Tm], abr[0:1, Tm]))
            vch(lambda e: e.tensor_sub(abr[0:1, Tv], abr[0:1, Te], abr[0:1, Tt]))
            stats_read.append(vch(lambda e: e.tensor_scalar_add(abr[0:1, Tv], abr[0:1, Tv], 1e-5)))
            t = em("scalar", lambda e: e.activation(abr[0:1, Tr], abr[0:1, Tv], AF.Sqrt),
                   waits=[(sV, sV.n)], inc=sS)
            vch(lambda e: e.reciprocal(abr[0:1, Ta], abr[0:1, Tr]), waits=[(sS, t)])
            vsm = vch(lambda e: e.scalar_tensor_tensor(abr[0:1, Tb], abr[0:1, Tm], -1.0, abr[0:1, Ta],
                                                       ALU.mult, ALU.mult))
            bw = [(sV, vsm), (sP, prologue_p)]
            if s >= 2:
                bw.append((sV, evac_done[s - 2]))
            em("tensor", lambda e, buf=buf: e.matmul(ps_ab[buf][:], onesf[0:1, 0:128],
                                                     abr[0:1, 4 * TSTR:6 * TSTR],
                                                     start=True, stop=True), waits=bw, inc=sP)
            abo = buf * 2 * TSTR
            aw = [(sP, sP.n)]
            em("vector", lambda e, buf=buf, abo=abo: e.tensor_copy(
                abc_sb[:, abo:abo + 2 * TSTR], ps_ab[buf][:]), waits=aw, inc=sV)
            evac_done.append(sV.n)
            vev = sV.n
            for kc in range(KC):
                lt = lntmp[:, (kc % 2) * TSTR:(kc % 2 + 1) * TSTR]
                em("vector", lambda e, kc=kc, xs=xs, abo=abo, lt=lt: e.tensor_mul(
                    lt[:], xs[:, kc * TSTR:(kc + 1) * TSTR], abc_sb[:, abo:abo + TSTR]))
                em("vector", lambda e, kc=kc, s=s, abo=abo, lt=lt: e.tensor_add(
                    xnT[:, kc * S + s * TSTR:kc * S + (s + 1) * TSTR], lt[:],
                    abc_sb[:, abo + TSTR:abo + 2 * TSTR]),
                   inc=(sV if kc == KC - 1 else None))
            apply_done.append(sV.n)
            if debug == 2 and s == 0:
                DMA("sync", dbg_abr[0:1, 0:7 * TSTR], abr[:, :], waits=[(sL, sL.n)])
                DMA("sync", dbg_abc[:, 0:4 * TSTR], abc_sb[:, :])
        ln_done = sL.n
        ln_v = sV.n
        ln_s = sS.n

    emit_gamma_units(20)

    wstage = EA(nc.sbuf_tensor("wstage", [128, 2048], F32))
    wq_bf = EA(nc.sbuf_tensor("wq_bf", [128, 2048], BF16))
    wk_bf = EA(nc.sbuf_tensor("wk_bf", [128, 2048], BF16))
    wv_bf = EA(nc.sbuf_tensor("wv_bf", [128, 2048], BF16))
    qT = EA(nc.sbuf_tensor("qT", [128, S], BF16))
    kT = EA(nc.sbuf_tensor("kT", [128, S], BF16))
    vA = EA(nc.sbuf_tensor("vA", [128, 16 * 129], BF16))
    q2 = EA(nc.sbuf_tensor("q2", [128, S], BF16))
    qsq_sb = EA(nc.sbuf_tensor("qsq_sb", [1, S], BF16))
    Aq = EA(nc.sbuf_tensor("Aq", [16, S], BF16))
    Bk = EA(nc.sbuf_tensor("Bk", [16, S], BF16))
    d_t = EA(nc.sbuf_tensor("d_t", [NSP, S], BF16))
    a1buf = EA(nc.sbuf_tensor("a1buf", [NSP, S], BF16))
    expb = EA(nc.sbuf_tensor("expb", [128, 2 * 512], BF16))
    o_std = EA(nc.sbuf_tensor("o_std", [128, 16 * 128], BF16))
    obl = EA(nc.sbuf_tensor("obl", [128, 2 * 128], BF16))
    gcolB = EA(nc.sbuf_tensor("gcolB", [128, 8], F32))
    wo_bf = EA(nc.sbuf_tensor("wo_bf", [128, H * 2048], BF16))

    ln_guard = [(sL, ln_done), (sV, ln_v), (sS, ln_s), (sP, sP.n)]
    em("vector", lambda e: e.memset(vA[:], 1.0), waits=ln_guard, inc=sV)

    proj_done = {}

    def weight_chain(h, first):
        wts = list(ln_guard) if first else [(sP, proj_done[h - 1])]
        d1 = DMA("sync", wstage[:, :], wq[h, :, :], sem=sW, waits=wts)
        c1 = em("vector", lambda e: e.tensor_copy(wq_bf[:], wstage[:, :]), waits=[(sW, d1)], inc=sV)
        d2 = DMA("sync", wstage[:, :], wk[h, :, :], sem=sW, waits=[(sV, c1)])
        c2 = em("vector", lambda e: e.tensor_copy(wk_bf[:], wstage[:, :]), waits=[(sW, d2)], inc=sV)
        d3 = DMA("sync", wstage[:, :], wv[h, :, :], sem=sW, waits=[(sV, c2)])
        c3 = em("vector", lambda e: e.tensor_copy(wv_bf[:], wstage[:, :]), waits=[(sW, d3)], inc=sV)
        return {"v": c3}

    wc = weight_chain(0, True)
    wo_ready = None
    prev_exp_bank = {}
    prev_av_bank = {}
    head_end_s = ln_s
    wpieces = []
    wchain_last = [None]

    def emit_wpiece():
        if wpieces:
            p = wpieces.pop(0)
            if p is not None:
                p()

    for h in range(H):
        with nc.psum_tensor([128, 512], F32) as ps_p0, \
             nc.psum_tensor([128, 512], F32) as ps_p1:
            ps_p = [ps_p0, ps_p1]
            evacs = {}
            base_w = [(sV, wc["v"]), (sL, ln_done), (sV, sV.n), (sS, head_end_s)]
            for (name, wbf, dst) in (("q", wq_bf, qT), ("k", wk_bf, kT)):
                for tc in range(4):
                    buf = tc % 2
                    mw = list(base_w) if (name == "q" and tc == 0) else []
                    prevkey = (name, tc - 2) if tc >= 2 else (("q", tc + 2) if name == "k" else None)
                    if prevkey in evacs:
                        mw.append((sV, evacs[prevkey]))
                    for kc in range(KC):
                        em("tensor", lambda e, kc=kc, tc=tc, buf=buf, wbf=wbf: e.matmul(
                            ps_p[buf][:], wbf[:, kc * 128:(kc + 1) * 128],
                            xnT[:, kc * S + tc * 512:kc * S + tc * 512 + 512],
                            start=(kc == 0), stop=(kc == KC - 1)),
                           waits=mw if kc == 0 else (), inc=(sP if kc == KC - 1 else None))
                    pmm = sP.n
                    evacs[(name, tc)] = em(
                        "vector", lambda e, tc=tc, buf=buf, dst=dst: e.tensor_copy(
                            dst[:, tc * 512:(tc + 1) * 512], ps_p[buf][:]),
                        waits=[(sP, pmm)], inc=sV)
            for tt in range(16):
                buf = tt % 2
                mw = []
                prevkey = ("v", tt - 2) if tt >= 2 else ("k", 2 + tt)
                if prevkey in evacs:
                    mw.append((sV, evacs[prevkey]))
                for kc in range(KC):
                    em("tensor", lambda e, kc=kc, tt=tt, buf=buf: e.matmul(
                        ps_p[buf][:, 0:128], xnT[:, kc * S + tt * 128:kc * S + (tt + 1) * 128],
                        wv_bf[:, kc * 128:(kc + 1) * 128],
                        start=(kc == 0), stop=(kc == KC - 1)),
                       waits=mw if kc == 0 else (), inc=(sP if kc == KC - 1 else None))
                pmm = sP.n
                evacs[("v", tt)] = em(
                    "vector", lambda e, tt=tt, buf=buf: e.tensor_copy(
                        vA[:, tt * 129:tt * 129 + 128], ps_p[buf][:, 0:128]),
                    waits=[(sP, pmm)], inc=sV)
            proj_v = sV.n
        proj_done[h] = sP.n

        wchain_last[0] = None

        def _wp(src_ap, dst, h=h):
            def go():
                prev = wchain_last[0]
                wts = [(sP, proj_done[h])] if prev is None else [(sV, prev)]
                d = DMA("sync", wstage[:, :], src_ap, sem=sW, waits=wts)
                wchain_last[0] = em("vector", lambda e, dst=dst: e.tensor_copy(dst, wstage[:, :]),
                                    waits=[(sW, d)], inc=sV)
            return go

        if h + 1 < H:
            wpieces = [None, None,
                       _wp(wq[h + 1, :, :], wq_bf[:]),
                       _wp(wk[h + 1, :, :], wk_bf[:]),
                       _wp(wv[h + 1, :, :], wv_bf[:])]
        else:
            wpieces = [_wp(wo[hh, :, :], wo_bf[:, hh * 2048:(hh + 1) * 2048])
                       for hh in range(H)]

        with nc.psum_tensor([128, 512], F32) as ps_sA0, \
             nc.psum_tensor([128, 512], F32) as ps_sA1, \
             nc.psum_tensor([128, 129], F32) as ps_oA0, \
             nc.psum_tensor([128, 129], F32) as ps_oA1, \
             nc.psum_tensor([128, 129], F32) as ps_oA2, \
             nc.psum_tensor([128, 129], F32) as ps_oA3, \
             nc.psum_tensor([1, 512], F32) as ps_q10, \
             nc.psum_tensor([40, 512], F32) as ps_gd0:
            ps_s = [ps_sA0, ps_sA1]
            ps_o = [ps_oA0, ps_oA1, ps_oA2, ps_oA3]

            def emit_side(nm, src, dstf, ball, is_q):
                sq2 = em("vector", lambda e, src=src: e.tensor_mul(q2[:], src[:], src[:]),
                         waits=[(sV, proj_v), (sP, sP.n)], inc=sV)
                qev = {}
                for tc in range(4):
                    mw = [(sV, sq2), (sS, sS.n), (sV, sV.n)]
                    if tc >= 1:
                        mw.append((sV, qev[tc - 1]))
                    em("tensor", lambda e, tc=tc: e.matmul(
                        ps_q10[:], onescol[:, 0:1], q2[:, tc * 512:(tc + 1) * 512],
                        start=True, stop=True), waits=mw, inc=sP)
                    pmm = sP.n
                    qev[tc] = em("vector", lambda e, tc=tc: e.tensor_copy(
                        qsq_sb[0:1, tc * 512:(tc + 1) * 512], ps_q10[:]),
                        waits=[(sP, pmm)], inc=sV)
                gd_ev = {}
                for tc in range(4):
                    mw = [(sV, qev[tc])]
                    if tc >= 1:
                        mw.append((sS, gd_ev[tc - 1]))
                    em("tensor", lambda e, tc=tc, src=src, h=h: e.matmul(
                        ps_gd0[0:NSP, :], pdT_all[:, h * 16:h * 16 + 8],
                        src[:, tc * 512:(tc + 1) * 512], start=True, stop=False), waits=mw)
                    em("tensor", lambda e, tc=tc: e.matmul(
                        ps_gd0[0:NSP, :], c8[0:1, 0:NSP],
                        qsq_sb[0:1, tc * 512:(tc + 1) * 512], start=False, stop=True))
                    em("tensor", lambda e, tc=tc, src=src, h=h: e.matmul(
                        ps_gd0[32:40, :], pdT_all[:, h * 16 + 8:h * 16 + 16],
                        src[:, tc * 512:(tc + 1) * 512], start=True, stop=True,
                        tile_position=(0, 32)), inc=sP)
                    pmm = sP.n
                    em("scalar", lambda e, tc=tc, dstf=dstf, ball=ball, h=h: e.activation(
                        dstf[0:NSP, tc * 512:(tc + 1) * 512], ps_gd0[0:NSP, :], AF.Exp,
                        bias=ball[0:NSP, h:h + 1], scale=rs2_all[0:NSP, h:h + 1]),
                        waits=[(sP, pmm)])
                    gd_ev[tc] = em("scalar", lambda e, tc=tc, h=h: e.activation(
                        d_t[0:NSP, tc * 512:(tc + 1) * 512], ps_gd0[32:40, :], AF.Tanh,
                        bias=bt_all[0:NSP, h:h + 1], scale=ones8[0:NSP, 0:1]), inc=sS)
                scev = sS.n
                if is_q:
                    a1 = vch(lambda e: e.scalar_tensor_tensor(
                        a1buf[0:NSP, :], Aq[0:NSP, :], c1col[0:NSP, 0:1], d_t[0:NSP, :],
                        ALU.mult, ALU.mult), waits=[(sV, prologue_v), (sS, scev), (sG, sG.n)])
                    vch(lambda e: e.tensor_single_scalar(
                        Aq[0:NSP, :], Aq[0:NSP, :], c0col[0:NSP, 0:1], ALU.mult))
                else:
                    a1 = vch(lambda e: e.tensor_mul(a1buf[0:NSP, :], Bk[0:NSP, :], d_t[0:NSP, :]),
                             waits=[(sS, scev), (sG, sG.n)])
                em("gpsimd", lambda e, dstf=dstf: e.dma_start(
                    out=dstf[NSP:16, :], in_=a1buf[0:NSP, :]), waits=[(sV, a1)], inc=sG, amt=16)

            def do_av(ii, nkc, exp_t, mask_t, evac_v, first_w=()):
                bb = ii % 2
                aw = [(sL, mask_t[ii])] if ii in mask_t else [(sS, exp_t[ii])]
                if ii == 0:
                    aw.append((sV, evac_v))
                    aw += list(first_w)
                for sq_i in range(4):
                    em("tensor", lambda e, bb=bb, ii=ii, sq_i=sq_i, nkc=nkc: e.matmul(
                        ps_o[sq_i][:], expb[:, bb * 512 + sq_i * 128:bb * 512 + (sq_i + 1) * 128],
                        vA[:, ii * 129:(ii + 1) * 129],
                        start=(ii == 0), stop=(ii == nkc - 1)),
                       waits=aw if sq_i == 0 else (), inc=(sP if sq_i == 3 else None))
                prev_av_bank[bb] = sP.n
                return sP.n

            def run_cells(qc, br, base_mw, first_w, pending=()):
                nkc = 4 * (qc + 1)
                qs = slice(qc * 512, (qc + 1) * 512)
                exp_t = {}
                mask_t = {}
                evac_v = sV.n
                for i in range(nkc):
                    if i == 2:
                        for f in pending:
                            f()
                    bank = i % 2
                    ks = slice(i * 128, (i + 1) * 128)
                    mw = list(base_mw)
                    if bank in prev_exp_bank:
                        mw.append((sS, prev_exp_bank[bank]))
                    if br == 0:
                        em("tensor", lambda e, bank=bank, ks=ks, qs=qs: e.matmul(
                            ps_s[bank][:], kT[:, ks], qT[:, qs], start=True, stop=True),
                           waits=mw, inc=sP)
                    else:
                        em("tensor", lambda e, bank=bank, ks=ks, qs=qs: e.matmul(
                            ps_s[bank][:], kT[:, ks], qT[:, qs], start=True, stop=False),
                           waits=mw)
                        em("tensor", lambda e, bank=bank, ks=ks, qs=qs: e.matmul(
                            ps_s[bank][:], Bk[0:16, ks], Aq[0:16, qs], start=False, stop=True),
                           inc=sP)
                    sc_t = sP.n
                    if i >= 1:
                        do_av(i - 1, nkc, exp_t, mask_t, evac_v, first_w)
                    ew = [(sP, sc_t)]
                    if bank in prev_av_bank:
                        ew.append((sP, prev_av_bank[bank]))
                    exp_t[i] = em("scalar", lambda e, bank=bank: e.activation(
                        expb[:, bank * 512:(bank + 1) * 512], ps_s[bank][:], AF.Exp,
                        bias=0.0, scale=ISQ), waits=ew, inc=sS)
                    prev_exp_bank[bank] = exp_t[i]
                    if i >= 4 * qc:
                        base = qc * 512 - i * 128
                        mask_t[i] = em("gpsimd", lambda e, bank=bank, base=base: e.affine_select(
                            expb[:, bank * 512:(bank + 1) * 512],
                            expb[:, bank * 512:(bank + 1) * 512],
                            [[1, 512]], ALU.is_ge, 0.0, base=base, channel_multiplier=-1),
                            waits=[(sS, exp_t[i])], inc=sL)
                return do_av(nkc - 1, nkc, exp_t, mask_t, evac_v, first_w)

            for qc in range(4):
                av_last = run_cells(qc, 0, [(sV, proj_v)], ())
                for sq_i in range(4):
                    oc = qc * 4 + sq_i
                    vch(lambda e, sq_i=sq_i, oc=oc: e.reciprocal(
                        gcolB[:, 0:1], ps_o[sq_i][:, 128:129]),
                        waits=[(sP, av_last)])
                    vch(lambda e: e.tensor_mul(
                        gcolB[:, 0:1], gcolB[:, 0:1], ombcol[:, 0:1]))
                    vch(lambda e, sq_i=sq_i, oc=oc: e.tensor_single_scalar(
                        o_std[:, oc * 128:(oc + 1) * 128], ps_o[sq_i][:, 0:128],
                        gcolB[:, 0:1], ALU.mult))
                if qc == 0:
                    emit_side("q", qT, Aq, bq_all, True)
                elif qc == 1:
                    emit_side("k", kT, Bk, bk_all, False)
                emit_wpiece()
            splat_v = sV.n
            splat_g = sG.n
            splat_s = sS.n

        with nc.psum_tensor([128, 512], F32) as ps_sB0, \
             nc.psum_tensor([128, 512], F32) as ps_sB1, \
             nc.psum_tensor([128, 129], F32) as ps_oB0, \
             nc.psum_tensor([128, 129], F32) as ps_oB1, \
             nc.psum_tensor([128, 129], F32) as ps_oB2, \
             nc.psum_tensor([128, 129], F32) as ps_oB3, \
             nc.psum_tensor([128, 128], BF16) as ps_tr0, \
             nc.psum_tensor([128, 128], BF16) as ps_tr1:
            ps_s = [ps_sB0, ps_sB1]
            ps_o = [ps_oB0, ps_oB1, ps_oB2, ps_oB3]
            ps_tr = [ps_tr0, ps_tr1]
            tr_ev = {}
            bmw = [(sV, splat_v), (sG, splat_g), (sS, splat_s)]
            pend = []
            for qc in range(4):
                av_last = run_cells(qc, 1, bmw, [(sS, splat_s)], pending=pend)
                pend = []
                for sq_i in range(4):
                    oc = qc * 4 + sq_i
                    slot = sq_i % 2
                    vch(lambda e, sq_i=sq_i: e.reciprocal(
                        gcolB[:, 4:5], ps_o[sq_i][:, 128:129]),
                        waits=[(sP, av_last)])
                    vch(lambda e: e.tensor_mul(
                        gcolB[:, 4:5], gcolB[:, 4:5], blcol[:, 0:1]))
                    ob = vch(lambda e, sq_i=sq_i, oc=oc, slot=slot: e.scalar_tensor_tensor(
                        obl[:, slot * 128:(slot + 1) * 128], ps_o[sq_i][:, 0:128],
                        gcolB[:, 4:5], o_std[:, oc * 128:(oc + 1) * 128],
                        ALU.mult, ALU.add))

                    def mk_tr(oc=oc, slot=slot, ob=ob, h=h):
                        def go():
                            tw = [(sV, ob)]
                            prev = tr_ev.get(slot)
                            if prev is not None:
                                tw.append((sV, prev))
                            tt_g = em("tensor", lambda e, slot=slot: e.transpose(
                                ps_tr[slot][:], obl[:, slot * 128:(slot + 1) * 128],
                                id_sb[:, :]), waits=tw, inc=sP)
                            tr_ev[slot] = em(
                                "vector", lambda e, oc=oc, h=h, slot=slot: e.tensor_copy(
                                    oT[:, h * S + oc * 128:h * S + (oc + 1) * 128],
                                    ps_tr[slot][:]), waits=[(sP, tt_g)], inc=sV)
                        return go
                    pend.append(mk_tr())
                emit_wpiece()
            for f in pend:
                f()
        wc = {"v": wchain_last[0]}
        if h + 1 == H:
            wo_ready = wchain_last[0]
        head_end_s = sS.n
        emit_gamma_units(14)

    emit_gamma_units(NGU)
    heads_v = sV.n
    heads_s = sS.n

    vch(lambda e: e.tensor_reduce(gcol3[:, 0:1], gacc[:, 0:2 * FC], AX.X, ALU.add),
        waits=[(sS, gamma_state["sc"][-1])])
    vch(lambda e: e.tensor_reduce(gcol3[:, 1:2], gacc[:, 2 * FC:4 * FC], AX.X, ALU.add))
    gv = vch(lambda e: e.tensor_reduce(gcol3[:, 2:3], gacc[:, 4 * FC:6 * FC], AX.X, ALU.add))
    gl = em("gpsimd", lambda e: e.tensor_reduce(grow3[0:1, 0:3], gcol3[:, 0:3], AX.C, ALU.add),
            waits=[(sV, gv)], inc=sL)
    gd = em("gpsimd", lambda e: e.dma_start(out=gam_in[:, :], in_=grow3[0:1, 0:3]),
            waits=[(sL, gl)], inc=sG, amt=16)
    em("gpsimd", lambda e: e.collective_compute(
        "AllReduce", ALU.add, replica_groups=ALL8,
        ins=[gam_in.ap().opt()], outs=[gam_out.ap().opt()]), waits=[(sG, gd)], inc=sC)
    gamma_coll = sC.n

    with nc.Block() as block:
        bld.flush(block)

    with nc.sbuf_tensor("attn_st", [128, 2 * 512], F32) as attn_st, \
         nc.psum_tensor("ps_w0", [128, 512], F32) as ps_w0, \
         nc.psum_tensor("ps_w1", [128, 512], F32) as ps_w1:
        ps_w = [ps_w0, ps_w1]
        ev = {}
        st_hist = {}
        last_store = None
        idx = 0
        for tt in range(16):
            for dc in range(4):
                buf = idx % 2
                mw = [(sV, wo_ready), (sV, heads_v), (sS, heads_s)]
                if idx >= 2:
                    mw.append((sV, ev[idx - 2]))
                for hh in range(H):
                    em("tensor", lambda e, hh=hh, tt=tt, dc=dc, buf=buf: e.matmul(
                        ps_w[buf][:], oT[:, hh * S + tt * 128:hh * S + (tt + 1) * 128],
                        wo_bf[:, hh * 2048 + dc * 512:hh * 2048 + (dc + 1) * 512],
                        start=(hh == 0), stop=(hh == H - 1)),
                       waits=mw if hh == 0 else (), inc=(sP if hh == H - 1 else None))
                pmm = sP.n
                evw = [(sP, pmm)]
                if idx >= 2:
                    evw.append(st_hist[idx - 2])
                ev[idx] = em("vector", lambda e, buf=buf: e.tensor_copy(
                    attn_st[:, buf * 512:(buf + 1) * 512], ps_w[buf][:]),
                    waits=evw, inc=sV)
                last_store = DMA("sync", part_secs[0][tt * 128:(tt + 1) * 128, dc * 512:(dc + 1) * 512],
                                 attn_st[:, buf * 512:(buf + 1) * 512], sem=sA[buf],
                                 waits=[(sV, ev[idx])])
                st_hist[idx] = (sA[buf], last_store)
                idx += 1
        em("gpsimd", lambda e: e.collective_compute(
            "ReduceScatter", ALU.add, replica_groups=PAIRS,
            ins=[part_secs[0].ap().opt()], outs=[rs_secs[0].ap().opt()]),
           waits=[(sA[0], sA[0].n), (sA[1], sA[1].n)], inc=sC)
        attn_coll = sC.n

        if debug == 2:
            DMA("sync", dbg_xn[:, :], xnT[:, 0:2048], waits=[(sC, rs1_coll)])
            DMA("sync", dbg_q[:, :], qT[:, :])
            DMA("sync", dbg_k[:, :], kT[:, :])
            DMA("sync", dbg_v[:, :], vA[:, :])
            DMA("sync", dbg_aq[:, :], Aq[:, :])
            DMA("sync", dbg_bk[:, :], Bk[:, :])
            DMA("sync", dbg_qsq[:, :], qsq_sb[:, :])
            DMA("sync", dbg_ot[:, :], oT[:, :])
            DMA("sync", dbg_expb[:, :], expb[:, :])
            DMA("sync", dbg_gb[:, :], gcolB[:, :])
            DMA("sync", dbg_ostd[:, :], o_std[:, 0:512])
            DMA("sync", dbg_pdt[:, :], pdT_all[:, :])
        with nc.Block() as block:
            bld.flush(block)
    esA.close()

    esC = ExitStack()
    EC = esC.enter_context
    hid = EC(nc.sbuf_tensor("hid", [128, FCG * S], BF16))
    w12stg = EC(nc.sbuf_tensor("w12stg", [128, 2 * 2048], F32))
    w1q = EC(nc.sbuf_tensor("w1q", [128, 2 * 2048], BF16))
    w2q = EC(nc.sbuf_tensor("w2q", [128, 2 * 2048], BF16))
    tmpneg = EC(nc.sbuf_tensor("tmpneg", [128, 1024], F32))
    thcol = EC(nc.sbuf_tensor("thcol", [128, 4], F32))
    nthcol = EC(nc.sbuf_tensor("nthcol", [128, 4], F32))
    sig_sb = EC(nc.sbuf_tensor("sig_sb", [128, 2 * 512], BF16))
    w3stg = EC(nc.sbuf_tensor("w3stg", [128, FCG * 512], F32))
    w3q = EC(nc.sbuf_tensor("w3q", [128, 2 * FCG * 512], BF16))
    fst = EC(nc.sbuf_tensor("fst", [128, 2 * 512], F32))
    th3 = EC(nc.sbuf_tensor("th3", [1, 8], F32))
    gamo_sb = EC(nc.sbuf_tensor("gamo_sb", [1, 4], F32))
    fin_f = EC(nc.sbuf_tensor("fin_f", [128, 2 * 2048], F32))
    fin_o = EC(nc.sbuf_tensor("fin_o", [128, 2 * 2048], F32))

    wo_guard = [(sP, sP.n), (sV, sV.n), (sS, sS.n), (sQ, sQ.n), (sL, sL.n)]
    gdr = em("scalar", lambda e: e.dma_start(out=gamo_sb[0:1, 0:3], in_=gam_out[:, :]),
             waits=[(sC, gamma_coll)] + wo_guard, inc=sSD, amt=16)
    vch(lambda e: e.tensor_scalar(th3[0:1, 0:3], gamo_sb[0:1, 0:3],
                                  1.0 / (8.0 * NW_ELEM), 5e-9, ALU.mult, ALU.add),
        waits=[(sSD, gdr)])
    thv = vch(lambda e: e.tensor_scalar_mul(th3[0:1, 4:7], th3[0:1, 0:3], -1.0))
    with nc.psum_tensor([128, 1], F32) as ps_th:
        for m in range(3):
            t = em("tensor", lambda e, m=m: e.matmul(ps_th[:, 0:1], onesf[0:1, 0:128],
                                                     th3[0:1, m:m + 1], start=True, stop=True),
                   waits=[(sV, thv), (sV, sV.n), (sS, sS.n)], inc=sP)
            vch(lambda e, m=m: e.tensor_copy(thcol[:, m:m + 1], ps_th[:, 0:1]),
                waits=[(sP, t)])
    th_v = vch(lambda e: e.tensor_scalar_mul(nthcol[:, 0:3], thcol[:, 0:3], -1.0))

    def quantize(dst, stg_ap, m, slot, waits, dst_war=None):
        qv = None
        for hh in range(2):
            w0 = list(waits) + [(sV, th_v), (sV, sV.n)] if hh == 0 else [(sV, sV.n)]
            pa = em("vector", lambda e, stg_ap=stg_ap, m=m, hh=hh: e.tensor_single_scalar(
                tmpneg[:, 0:1024], stg_ap[:, hh * 1024:(hh + 1) * 1024],
                nthcol[0:128, m:m + 1], ALU.is_le), waits=w0, inc=sV)
            qw = [(sV, pa)]
            if dst_war is not None and hh == 0:
                qw.append((sP, dst_war))
            qv = em("vector", lambda e, dst=dst, stg_ap=stg_ap, m=m, hh=hh: e.scalar_tensor_tensor(
                dst[:, hh * 1024:(hh + 1) * 1024], stg_ap[:, hh * 1024:(hh + 1) * 1024],
                thcol[:, m:m + 1], tmpneg[:, 0:1024],
                ALU.is_ge, ALU.subtract),
                waits=qw, inc=sV)
        return qv

    prev_grp_qv = None
    prev_w3_qv = None
    fdma = {}
    w3ev = {}
    w3git = [0]
    goff_list = [0, 8, 15]
    w3off_list = [0, 8 * 4 * 512, 15 * 4 * 512]
    for g in range(NGRP):
        gsz = GRPS[g]
        goff = goff_list[g]
        with nc.psum_tensor([128, 512], F32) as ps_xa0, \
             nc.psum_tensor([128, 512], F32) as ps_xa1, \
             nc.psum_tensor([128, 512], F32) as ps_xa2, \
             nc.psum_tensor([128, 512], F32) as ps_xb0, \
             nc.psum_tensor([128, 512], F32) as ps_xb1, \
             nc.psum_tensor([128, 512], F32) as ps_xb2:
            ps_xa = [ps_xa0, ps_xa1, ps_xa2]
            ps_xb = [ps_xb0, ps_xb1, ps_xb2]
            q_prev = {}
            q_prev2 = {}
            hid_v = {}
            slot_mm = {}
            sg_prev = None
            for fi in range(gsz):
                fc = goff + fi
                slot = fi % 2
                if fi >= 1:
                    wts1 = [(sV, q_prev[fi - 1])]
                elif g == 0:
                    wts1 = list(wo_guard)
                else:
                    wts1 = [(sV, prev_grp_qv)]
                d1 = DMA("sync", w12stg[:, 0:2048], w1[fc, :, :], sem=sW1, waits=wts1)
                qv1 = quantize(w1q[:, slot * 2048:(slot + 1) * 2048],
                               w12stg[:, 0:2048], 0, 0, [(sW1, d1)],
                               dst_war=slot_mm.get(slot))
                d2 = DMA("sync", w12stg[:, 2048:4096], w2[fc, :, :], sem=sW2,
                         waits=([(sV, q_prev2[fi - 1])] if fi >= 1 else
                                ([(sV, prev_grp_qv)] if g > 0 else [])))
                qv2 = quantize(w2q[:, slot * 2048:(slot + 1) * 2048],
                               w12stg[:, 2048:4096], 1, 0, [(sW2, d2)],
                               dst_war=slot_mm.get(slot))
                q_prev[fi] = qv1
                q_prev2[fi] = qv2
                for tc in range(4):
                    buf = (fi * 4 + tc) % 3
                    mw = [(sV, qv1), (sV, qv2)]
                    if safe >= 2:
                        mw += allw()
                    gi = fi * 4 + tc - 3
                    prev = hid_v.get((gi // 4, gi % 4)) if gi >= 0 else None
                    if prev is not None:
                        mw.append((sV, prev))
                    if g > 0 and fi == 0 and tc < 2:
                        mw.append((sV, sV.n))
                    for kc in range(KC):
                        em("tensor", lambda e, kc=kc, tc=tc, buf=buf, slot=slot: e.matmul(
                            ps_xa[buf][:], w1q[:, slot * 2048 + kc * 128:slot * 2048 + (kc + 1) * 128],
                            xnT[:, kc * S + tc * 512:kc * S + tc * 512 + 512],
                            start=(kc == 0), stop=(kc == KC - 1)),
                           waits=mw if kc == 0 else ())
                    for kc in range(KC):
                        em("tensor", lambda e, kc=kc, tc=tc, buf=buf, slot=slot: e.matmul(
                            ps_xb[buf][:], w2q[:, slot * 2048 + kc * 128:slot * 2048 + (kc + 1) * 128],
                            xnT[:, kc * S + tc * 512:kc * S + tc * 512 + 512],
                            start=(kc == 0), stop=(kc == KC - 1)),
                           inc=(sP if kc == KC - 1 else None))
                    pmm = sP.n
                    ss = tc % 2
                    sgw = [(sP, pmm)]
                    if sg_prev is not None:
                        sgw.append((sV, sg_prev))
                    if safe >= 2:
                        sgw += allw()
                    sg = em("scalar", lambda e, buf=buf, ss=ss: e.activation(
                        sig_sb[:, ss * 512:(ss + 1) * 512], ps_xa[buf][:], AF.Sigmoid),
                        waits=sgw, inc=sS)
                    hid_v[(fi, tc)] = em("vector", lambda e, fi=fi, tc=tc, buf=buf, ss=ss: e.tensor_mul(
                        hid[:, fi * S + tc * 512:fi * S + (tc + 1) * 512], ps_xb[buf][:],
                        sig_sb[:, ss * 512:(ss + 1) * 512]),
                        waits=[(sS, sg)], inc=sV)
                    gi2 = fi * 4 + tc - 2
                    sg_prev = hid_v.get((gi2 // 4, gi2 % 4)) if gi2 >= 0 else None
                slot_mm[slot] = sP.n
            prev_grp_qv = q_prev2[gsz - 1]
            hid_done = sV.n

        with nc.psum_tensor([128, 512], F32) as ps_f0, \
             nc.psum_tensor([128, 512], F32) as ps_f1:
            ps_f = [ps_f0, ps_f1]
            w3mm_prev = {}
            last_grp = (g == NGRP - 1)
            full_order = [t for q in range(4) for t in (2 * q, 2 * q + 8, 2 * q + 1, 2 * q + 9)]
            for (pi, dc) in enumerate(range(4)):
                half = 0
                qslot = pi % 2
                qoff = qslot * FCG * 512
                w3qv = {}
                wts = [(sV, hid_done)]
                if prev_w3_qv is not None:
                    wts.append((sV, prev_w3_qv))
                woff = w3off_list[g] + dc * gsz * 512
                d = DMA("sync", w3stg[:, 0:gsz * 512], w3[:, woff:woff + gsz * 512],
                        sem=sW3[0], waits=wts)
                for fi in range(gsz):
                    pw = [(sW3[0], d)]
                    stw = []
                    if w3mm_prev.get(qslot) is not None:
                        stw.append((sP, w3mm_prev[qslot]))
                    if safe >= 1:
                        stw += allw()
                    pa = em("vector", lambda e, fi=fi: e.tensor_single_scalar(
                        tmpneg[:, 0:512], w3stg[:, fi * 512:(fi + 1) * 512],
                        nthcol[0:128, 2:3], ALU.is_le), waits=pw + [(sV, sV.n)], inc=sV)
                    w3qv[fi] = em("vector", lambda e, fi=fi, qoff=qoff: e.scalar_tensor_tensor(
                        w3q[:, qoff + fi * 512:qoff + (fi + 1) * 512],
                        w3stg[:, fi * 512:(fi + 1) * 512],
                        thcol[:, 2:3], tmpneg[:, 0:512], ALU.is_ge, ALU.subtract),
                        waits=stw + [(sV, pa)], inc=sV)
                w3q_ready = sV.n
                prev_w3_qv = w3qv[gsz - 1]
                tts = full_order if last_grp else list(range(16))
                for (li, tt) in enumerate(tts):
                    buf = w3git[0] % 2
                    mw = [(sV, w3q_ready)]
                    if w3git[0] - 2 in w3ev:
                        mw.append((sV, w3ev[w3git[0] - 2]))
                    if safe >= 1:
                        mw += allw()
                    for fi in range(gsz):
                        em("tensor", lambda e, fi=fi, tt=tt, buf=buf, gsz=gsz, qoff=qoff: e.matmul(
                            ps_f[buf][:], hid[:, fi * S + tt * 128:fi * S + (tt + 1) * 128],
                            w3q[:, qoff + fi * 512:qoff + (fi + 1) * 512],
                            start=(fi == 0), stop=(fi == gsz - 1)),
                           waits=mw if fi == 0 else (), inc=(sP if fi == gsz - 1 else None))
                    pmm = sP.n
                    evw = [(sP, pmm)]
                    prevst = fdma.get(w3git[0] - 2)
                    if prevst is not None:
                        evw.append(prevst)
                    if safe >= 1:
                        evw += allw()
                    w3ev[w3git[0]] = em("vector", lambda e, buf=buf: e.tensor_copy(
                        fst[:, buf * 512:(buf + 1) * 512], ps_f[buf][:]), waits=evw, inc=sV)
                    row0 = tt * 128
                    dlast = DMA("sync",
                                part_secs[1 + g][row0:row0 + 128, dc * 512:(dc + 1) * 512],
                                fst[:, buf * 512:(buf + 1) * 512], sem=sF[buf],
                                waits=[(sV, w3ev[w3git[0]])])
                    fdma[w3git[0]] = (sF[buf], dlast)
                    w3git[0] += 1
                w3mm_prev[qslot] = sP.n
                if dc == 3:
                    em("gpsimd", lambda e, g=g: e.collective_compute(
                        "ReduceScatter", ALU.add, replica_groups=PAIRS,
                        ins=[part_secs[1 + g].ap().opt()], outs=[rs_secs[1 + g].ap().opt()]),
                       waits=[(sF[0], sF[0].n), (sF[1], sF[1].n)], inc=sC)

    rs_coll = sC.n

    st1 = {}
    prev_adds = {}
    for tt in range(8):
        osl = tt % 2
        dxw = [st1[tt - 2]] if tt >= 2 else [(sC, attn_coll + NGRP - 1)]
        dx = DMA("sync", fin_o[:, osl * 2048:(osl + 1) * 2048],
                 xOwn[tt * 128:(tt + 1) * 128, :], waits=dxw)
        vv = None
        for j in range(NGRP):
            fs = j % 2
            fw = [(sC, attn_coll + j)]
            pa = prev_adds.get(fs)
            if pa is not None:
                fw.append((sV, pa))
            df = DMA("sync", fin_f[:, fs * 2048:(fs + 1) * 2048],
                     rs_secs[j][tt * 128:(tt + 1) * 128, :],
                     sem=sX[fs], waits=fw)
            vv = em("vector", lambda e, fs=fs, osl=osl: e.tensor_add(
                fin_o[:, osl * 2048:(osl + 1) * 2048], fin_o[:, osl * 2048:(osl + 1) * 2048],
                fin_f[:, fs * 2048:(fs + 1) * 2048]),
                waits=[(sX[fs], df), (sQ, dx), (sV, sV.n)], inc=sV)
            prev_adds[fs] = vv
        d = DMA("sync", acc_sec[tt * 128:(tt + 1) * 128, :],
                fin_o[:, osl * 2048:(osl + 1) * 2048], sem=sF[osl], waits=[(sV, vv)])
        st1[tt] = (sF[osl], d)

    n_out = 0
    prev2 = dict(prev_adds)
    for tt in range(8):
        osl = tt % 2
        daw = [st1[6 + osl]]
        if tt >= 2:
            daw.append((sO, 16 * (tt - 1)))
        da = DMA("sync", fin_o[:, osl * 2048:(osl + 1) * 2048],
                 acc_sec[tt * 128:(tt + 1) * 128, :], waits=daw)
        fs = tt % 2
        fw = [(sC, attn_coll + NGRP)]
        pa = prev2.get(fs)
        if pa is not None:
            fw.append((sV, pa))
        df = DMA("sync", fin_f[:, fs * 2048:(fs + 1) * 2048],
                 rs_secs[NGRP][tt * 128:(tt + 1) * 128, :],
                 sem=sX[fs], waits=fw)
        vv = em("vector", lambda e, fs=fs, osl=osl: e.tensor_add(
            fin_o[:, osl * 2048:(osl + 1) * 2048], fin_o[:, osl * 2048:(osl + 1) * 2048],
            fin_f[:, fs * 2048:(fs + 1) * 2048]),
            waits=[(sX[fs], df), (sQ, da), (sV, sV.n)], inc=sV)
        prev2[fs] = vv
        em("sync", lambda e, tt=tt, osl=osl: e.dma_start(
            out=out[tt * 128:(tt + 1) * 128, :], in_=fin_o[:, osl * 2048:(osl + 1) * 2048]),
           waits=[(sV, vv)], inc=sO, amt=16)
        n_out += 1

    if debug:
        em("vector", lambda e: e.tensor_copy(gamo_sb[0:1, 3:4], grow3[0:1, 0:1]), inc=sV)
        dbv = sV.n
        DMA("sync", dbg_rows[0:1, 0:4], gamo_sb[0:1, 0:4], waits=[(sV, dbv), (sO, 16 * n_out)])
        DMA("sync", dbg_rows[0:1, 4:12], th3[0:1, 0:8])
        DMA("sync", dbg_ars[:, :], rs_secs[0][:, :])
        DMA("sync", dbg_hid[:, :], hid[:, :])
        DMA("sync", dbg_w3q[:, :], w3q[:, :])
        for g in range(NGRP):
            DMA("sync", dbg_frs[:, g * 1024:g * 1024 + 1024],
                rs_secs[1 + g][0:1024, 0:1024])
        DMA("sync", dbg_fp[:, 0:1024], part_secs[1][:, 0:1024])
        em("vector", lambda e: e.tensor_copy(fin_o[:, 0:4], thcol[:, :]),
           waits=[(sO, 16 * n_out)])
        em("vector", lambda e: e.tensor_copy(fin_o[:, 4:8], nthcol[:, :]))
        em("vector", lambda e: e.tensor_copy(fin_o[:, 8:9], blcol[:, :]))
        em("vector", lambda e: e.tensor_copy(fin_o[:, 9:10], ombcol[:, :]), inc=sV)
        DMA("sync", dbg_cols[:, :], fin_o[:, 0:16], waits=[(sV, sV.n)])
        em("sync", lambda e: e.nop(), waits=[(sQ, sQ.n)])
    em("sync", lambda e: e.nop(), waits=[(sO, 16 * n_out)])

    with nc.Block() as block:
        bld.flush(block)
    esC.close()
    es0.close()
    return nc


_NC_CACHE = {}


def prep_inputs(inputs):
    x = np.ascontiguousarray(np.asarray(inputs["L0_attn_hidden_states"], np.float32))
    Wq = np.asarray(inputs["L0_attn_Wq"], np.float32)
    Wk = np.asarray(inputs["L0_attn_Wk"], np.float32)
    Wv = np.asarray(inputs["L0_attn_Wv"], np.float32)
    Wo = np.asarray(inputs["L0_attn_Wo"], np.float32)
    spos = np.asarray(inputs["L0_attn_splat_positions"], np.float32)
    sdir = np.asarray(inputs["L0_attn_splat_directions"], np.float32)
    sls = np.asarray(inputs["L0_attn_splat_log_scales"], np.float32)
    sla = np.asarray(inputs["L0_attn_splat_log_amplitudes"], np.float32)
    ds = np.float32(inputs["L0_attn_directional_strength"])
    ps = np.float32(inputs["L0_attn_position_sensitivity"])
    gs = np.float32(inputs["L0_attn_gsa_strength"])
    w1f = np.asarray(inputs["L0_ffn_w1"], np.float32)
    w2f = np.asarray(inputs["L0_ffn_w2"], np.float32)
    w3f = np.asarray(inputs["L0_ffn_w3"], np.float32)

    def blockT(w_rows):
        nb = w_rows.shape[0] // 128
        t = w_rows.reshape(nb, 128, 16, 128)
        return np.ascontiguousarray(t.transpose(0, 3, 2, 1).reshape(nb, 128, 2048))

    scalv = np.array([[ds, ps, gs, 0.0]], np.float32)
    share = {}
    for b in range(4):
        xt = x[b].T
        share[b] = np.ascontiguousarray(
            xt.reshape(16, 128, S // TSTR, TSTR).transpose(1, 2, 0, 3).reshape(128, 16 * S))
    wq_b = blockT(Wq)
    wk_b = blockT(Wk)
    wv_b = blockT(Wv)
    woT = np.ascontiguousarray(Wo.T.reshape(16, 128, 2048))
    w1_b = blockT(w1f)
    w2_b = blockT(w2f)
    w3T = w3f.T

    w3_maps = {}
    for p in range(2):
        my = w3T[p * FH:(p + 1) * FH].reshape(22, 128, 4, 512)
        parts = []
        for (goff, gsz) in ((0, 8), (8, 7), (15, 7)):
            for dc in range(4):
                blk = my[goff:goff + gsz, :, dc, :]
                parts.append(blk.transpose(1, 0, 2).reshape(128, gsz * 512))
        w3_maps[p] = np.ascontiguousarray(np.concatenate(parts, axis=1))
    in_maps = []
    for c in range(8):
        b, p = c // 2, c % 2
        mh = slice(p * 8, (p + 1) * 8)
        mf = slice(p * FC, (p + 1) * FC)
        in_maps.append({
            "xTb": share[b],
            "xOwn": np.ascontiguousarray(x[b, p * 1024:(p + 1) * 1024]),
            "wq": np.ascontiguousarray(wq_b[mh]),
            "wk": np.ascontiguousarray(wk_b[mh]),
            "wv": np.ascontiguousarray(wv_b[mh]),
            "wo": np.ascontiguousarray(woT[mh]),
            "w1": np.ascontiguousarray(w1_b[mf]),
            "w2": np.ascontiguousarray(w2_b[mf]),
            "w3": w3_maps[p],
            "pos": np.ascontiguousarray(spos[mh].transpose(1, 0, 2).reshape(NSP, H * HD)),
            "dirs": np.ascontiguousarray(sdir[mh].transpose(1, 0, 2).reshape(NSP, H * HD)),
            "lsT": np.ascontiguousarray(sls[mh].T),
            "laT": np.ascontiguousarray(sla[mh].T),
            "scal": scalv,
        })

    return in_maps


def kernel(**inputs):
    import os
    in_maps = prep_inputs(inputs)
    safe = int(os.environ.get("KSAFE", "0"))
    key = ("nc", safe)
    if key not in _NC_CACHE:
        _NC_CACHE[key] = build(safe=safe)
    nc = _NC_CACHE[key]
    res = run_bass_kernel_spmd(nc, in_maps, core_ids=list(range(8)))
    outf = np.empty((4, S, DM), np.float32)
    for c in range(8):
        b, p = c // 2, c % 2
        outf[b, p * 1024:(p + 1) * 1024] = np.asarray(res.results[c]["out"], np.float32)
    return outf


# version 34
# speedup vs baseline: 1.1521x; 1.1521x over previous
import numpy as np
import ml_dtypes
import concourse.bass as bass
import concourse.mybir as mybir
from concourse.bass_utils import run_bass_kernel_spmd

F32 = mybir.dt.float32
BF16 = mybir.dt.bfloat16
AF = mybir.ActivationFunctionType
ALU = mybir.AluOpType
AX = mybir.AxisListType

S = 2048
DM = 2048
H = 8
HD = 128
NSP = 8
FH = 2816
FC = 22
KC = 16
NGRP = 3
GRPS = [8, 7, 7]
FCG = 8
TSTR = 256
NSTR = S // TSTR
ISQ = float(1.0 / np.sqrt(np.float32(128.0)))
SQH = float(np.sqrt(np.float32(128.0)))
NW_ELEM = 5632 * 2048

ENGS = ("sync", "vector", "tensor", "scalar", "gpsimd")
PAIRS = [[0, 1], [2, 3], [4, 5], [6, 7]]
ALL8 = [[0, 1, 2, 3, 4, 5, 6, 7]]


class Sem:
    __slots__ = ("h", "n", "name")

    def __init__(self, h, name):
        self.h = h
        self.n = 0
        self.name = name


class B:
    def __init__(self):
        self.ops = {e: [] for e in ENGS}
        self.waited = {}

    def emit(self, eng, fn, waits=(), inc=None, amt=1):
        rw = []
        for wv in waits:
            if wv is None:
                continue
            (s, v) = wv
            if v is None or v <= 0:
                continue
            key = (eng, s.name)
            if self.waited.get(key, -1) >= v:
                continue
            self.waited[key] = v
            rw.append((s, v))
        if inc is not None:
            inc.n += amt
        tgt = inc.n if inc is not None else None

        def run(e, fn=fn, rw=tuple(rw), inc=inc, amt=amt):
            for (s, v) in rw:
                e.wait_ge(s.h, v)
            r = fn(e)
            if inc is not None:
                r.then_inc(inc.h, amt)
        self.ops[eng].append(run)
        return tgt

    def flush(self, block):
        ops = self.ops
        self.ops = {e: [] for e in ENGS}

        @block.sync
        def _(e):
            for f in ops["sync"]:
                f(e)

        @block.vector
        def _(e):
            for f in ops["vector"]:
                f(e)

        @block.tensor
        def _(e):
            for f in ops["tensor"]:
                f(e)

        @block.scalar
        def _(e):
            for f in ops["scalar"]:
                f(e)

        @block.gpsimd
        def _(e):
            for f in ops["gpsimd"]:
                f(e)


def build(debug=False, safe=0):
    nc = bass.Bass("TRN2", target_bir_lowering=False)
    dt = nc.dram_tensor

    xTb = dt("xTb", [128, KC * S], F32, kind="ExternalInput")
    xOwn = dt("xOwn", [1024, DM], F32, kind="ExternalInput")
    wq = dt("wq", [H, 128, KC * 128], F32, kind="ExternalInput")
    wk = dt("wk", [H, 128, KC * 128], F32, kind="ExternalInput")
    wv = dt("wv", [H, 128, KC * 128], F32, kind="ExternalInput")
    wo = dt("wo", [H, 128, DM], F32, kind="ExternalInput")
    w1 = dt("w1", [FC, 128, KC * 128], F32, kind="ExternalInput")
    w2 = dt("w2", [FC, 128, KC * 128], F32, kind="ExternalInput")
    w3 = dt("w3", [128, 45056], F32, kind="ExternalInput")
    pos = dt("pos", [NSP, H * HD], F32, kind="ExternalInput")
    dirs = dt("dirs", [NSP, H * HD], F32, kind="ExternalInput")
    lsT = dt("lsT", [NSP, H], F32, kind="ExternalInput")
    laT = dt("laT", [NSP, H], F32, kind="ExternalInput")
    scal = dt("scal", [1, 4], F32, kind="ExternalInput")

    out = dt("out", [1024, DM], F32, kind="ExternalOutput")
    if debug:
        dbg_xn = dt("dbg_xn", [128, 2048], BF16, kind="ExternalOutput")
        dbg_q = dt("dbg_q", [128, S], BF16, kind="ExternalOutput")
        dbg_k = dt("dbg_k", [128, S], BF16, kind="ExternalOutput")
        dbg_v = dt("dbg_v", [128, 16 * 129], BF16, kind="ExternalOutput")
        dbg_aq = dt("dbg_aq", [16, S], BF16, kind="ExternalOutput")
        dbg_bk = dt("dbg_bk", [16, S], BF16, kind="ExternalOutput")
        dbg_qsq = dt("dbg_qsq", [1, S], BF16, kind="ExternalOutput")
        dbg_ot = dt("dbg_ot", [128, H * S], BF16, kind="ExternalOutput")
        dbg_expb = dt("dbg_expb", [128, 1024], BF16, kind="ExternalOutput")
        dbg_gb = dt("dbg_gb", [128, 8], F32, kind="ExternalOutput")
        dbg_ostd = dt("dbg_ostd", [128, 512], F32, kind="ExternalOutput")
        dbg_rows = dt("dbg_rows", [1, 16], F32, kind="ExternalOutput")
        dbg_ars = dt("dbg_ars", [1024, DM], F32, kind="ExternalOutput")
        dbg_hid = dt("dbg_hid", [128, FCG * S], BF16, kind="ExternalOutput")
        dbg_w3q = dt("dbg_w3q", [128, FCG * 512], BF16, kind="ExternalOutput")
        dbg_frs = dt("dbg_frs", [1024, NGRP * 1024], F32, kind="ExternalOutput")
        dbg_pdt = dt("dbg_pdt", [128, H * 16], BF16, kind="ExternalOutput")
        dbg_cols = dt("dbg_cols", [128, 16], F32, kind="ExternalOutput")
        dbg_abr = dt("dbg_abr", [1, 2048], F32, kind="ExternalOutput")
        dbg_fp = dt("dbg_fp", [S, 1024], F32, kind="ExternalOutput")
        dbg_xs = dt("dbg_xs", [128, 4096], F32, kind="ExternalOutput")
        dbg_abc = dt("dbg_abc", [128, 1024], F32, kind="ExternalOutput")

    part_secs = [dt(f"part_sec{j}", [S, DM], F32) for j in range(1 + NGRP)]
    rs_secs = [dt(f"rs_sec{j}", [1024, DM], F32) for j in range(1 + NGRP)]
    gam_in = dt("gam_in", [1, 3], F32)
    gam_out = dt("gam_out", [1, 3], F32)

    ident_h = nc.inline_tensor(np.eye(128, dtype=ml_dtypes.bfloat16), "ident")

    from contextlib import ExitStack
    es0 = ExitStack()
    E = es0.enter_context

    sQ = Sem(E(nc.semaphore("sQ")), "sQ")
    sG = Sem(E(nc.semaphore("sG")), "sG")
    sC = Sem(E(nc.semaphore("sC")), "sC")
    sP = Sem(E(nc.semaphore("sP")), "sP")
    sS = Sem(E(nc.semaphore("sS")), "sS")
    sV = Sem(E(nc.semaphore("sV")), "sV")
    sL = Sem(E(nc.semaphore("sL")), "sL")
    sO = Sem(E(nc.semaphore("sO")), "sO")
    sSD = Sem(E(nc.semaphore("sSD")), "sSD")
    sX = [Sem(E(nc.semaphore(f"sX{i}")), f"sX{i}") for i in range(2)]
    sGU = [Sem(E(nc.semaphore(f"sGU{i}")), f"sGU{i}") for i in range(2)]
    sW = Sem(E(nc.semaphore("sW")), "sW")
    sW1 = Sem(E(nc.semaphore("sW1")), "sW1")
    sW2 = Sem(E(nc.semaphore("sW2")), "sW2")
    sW3 = [Sem(E(nc.semaphore(f"sW3{i}")), f"sW3{i}") for i in range(2)]
    sF = [Sem(E(nc.semaphore(f"sF{i}")), f"sF{i}") for i in range(2)]
    sA = [Sem(E(nc.semaphore(f"sA{i}")), f"sA{i}") for i in range(2)]
    ALLS0 = [sQ, sG, sP, sS, sV, sL, sSD, sW, sW1, sW2] + sX + sGU + sW3 + sF + sA

    xnT = E(nc.sbuf_tensor("xnT", [128, KC * S], BF16))
    id_sb = E(nc.sbuf_tensor("id_sb", [128, 128], BF16))
    onesf = E(nc.sbuf_tensor("onesf", [1, 128], F32))
    onescol = E(nc.sbuf_tensor("onescol", [128, 1], BF16))
    c8 = E(nc.sbuf_tensor("c8", [1, NSP], BF16))
    ones8 = E(nc.sbuf_tensor("ones8", [NSP, 1], F32))
    lsT_sb = E(nc.sbuf_tensor("lsT_sb", [NSP, H], F32))
    laT_sb = E(nc.sbuf_tensor("laT_sb", [NSP, H], F32))
    scal_sb = E(nc.sbuf_tensor("scal_sb", [1, 4], F32))
    rs2_all = E(nc.sbuf_tensor("rs2_all", [NSP, H], F32))
    bq_all = E(nc.sbuf_tensor("bq_all", [NSP, H], F32))
    bk_all = E(nc.sbuf_tensor("bk_all", [NSP, H], F32))
    bt_all = E(nc.sbuf_tensor("bt_all", [NSP, H], F32))
    pdT_all = E(nc.sbuf_tensor("pdT_all", [128, H * 16], BF16))
    c0col = E(nc.sbuf_tensor("c0col", [NSP, 1], F32))
    c1col = E(nc.sbuf_tensor("c1col", [NSP, 1], F32))
    blcol = E(nc.sbuf_tensor("blcol", [128, 1], F32))
    ombcol = E(nc.sbuf_tensor("ombcol", [128, 1], F32))
    sm01 = E(nc.sbuf_tensor("sm01", [1, 4], F32))
    gacc = E(nc.sbuf_tensor("gacc", [128, 136], F32))
    gcol3 = E(nc.sbuf_tensor("gcol3", [128, 4], F32))
    grow3 = E(nc.sbuf_tensor("grow3", [1, 4], F32))

    bld = B()
    em = bld.emit

    def DMA(eng, dst, src, sem=sQ, waits=()):
        return em(eng, lambda e, d=dst, s=src: e.dma_start(out=d, in_=s),
                  waits=waits, inc=sem, amt=16)

    ALLS = ALLS0

    def allw():
        return [(s, s.n) for s in ALLS]

    def vch(fn, waits=()):
        return em("vector", fn, waits=list(waits) + [(sV, sV.n)], inc=sV)

    gamma_state = {"j": 0, "sc": []}
    NGU = 6 * FC

    def emit_gamma_units(n):
        for _ in range(n):
            j = gamma_state["j"]
            if j >= NGU:
                return
            gamma_state["j"] = j + 1
            m, blk2 = j // (2 * FC), j % (2 * FC)
            blk, half = blk2 // 2, blk2 % 2
            if m == 0:
                src = w1[blk, :, half * 1024:(half + 1) * 1024]
            elif m == 1:
                src = w2[blk, :, half * 1024:(half + 1) * 1024]
            else:
                cc = (2 * blk + half) * 1024
                src = w3[:, cc:cc + 1024]
            buf = j % 2
            wts = [(sS, gamma_state["sc"][j - 2])] if j >= 2 else []
            d = DMA("sync", gstage[:, buf * 1024:(buf + 1) * 1024], src, sem=sGU[buf], waits=wts)
            t = em("scalar",
                   lambda e, buf=buf, j=j: e.activation(gdump[:, :], gstage[:, buf * 1024:(buf + 1) * 1024],
                                                        AF.Abs, accum_out=gacc[:, j:j + 1]),
                   waits=[(sGU[buf], d), (sS, sS.n)], inc=sS)
            gamma_state["sc"].append(t)

    esA = ExitStack()
    EA = esA.enter_context
    oT = EA(nc.sbuf_tensor("oT", [128, H * S], BF16))
    gdump = EA(nc.sbuf_tensor("gdump", [128, 1024], BF16))
    gstage = EA(nc.sbuf_tensor("gstage", [128, 2 * 1024], F32))

    DMA("sync", lsT_sb[:, :], lsT[:, :])
    DMA("sync", laT_sb[:, :], laT[:, :])
    DMA("sync", scal_sb[:, :], scal[:, :])
    DMA("sync", id_sb[:, :], ident_h[:, :])

    em("vector", lambda e: e.memset(onesf[:], 1.0))
    em("vector", lambda e: e.memset(onescol[:], 1.0))
    em("vector", lambda e: e.memset(ones8[:], 1.0))
    em("vector", lambda e: e.memset(c8[:], -0.5), inc=sV)

    with nc.sbuf_tensor("pos_sb", [NSP, H * HD], F32) as pos_sb, \
         nc.sbuf_tensor("dirs_sb", [NSP, H * HD], F32) as dirs_sb, \
         nc.sbuf_tensor("dn_sb", [NSP, H * HD], F32) as dn_sb, \
         nc.sbuf_tensor("sc8a", [NSP, 4], F32) as sc8a, \
         nc.sbuf_tensor("sc8b", [NSP, 128], F32) as sc8b, \
         nc.sbuf_tensor("pbf", [NSP, 128], BF16) as pbf, \
         nc.sbuf_tensor("dnbf", [NSP, 128], BF16) as dnbf, \
         nc.psum_tensor("ps_t8", [128, NSP], BF16) as ps_t8, \
         nc.psum_tensor("ps_c", [128, 1], F32) as ps_c:
        DMA("sync", pos_sb[:, :], pos[:, :])
        w_in0 = DMA("sync", dirs_sb[:, :], dirs[:, :])
        for h in range(H):
            hs = slice(h * HD, (h + 1) * HD)
            vch(lambda e, hs=hs: e.tensor_mul(sc8b[:, :], dirs_sb[0:NSP, hs], dirs_sb[0:NSP, hs]),
                waits=[(sQ, w_in0)])
            vv = vch(lambda e: e.tensor_reduce(sc8a[:, 0:1], sc8b[:, :], AX.X, ALU.add))
            t = em("scalar", lambda e: e.activation(sc8a[:, 1:2], sc8a[:, 0:1], AF.Sqrt),
                   waits=((sV, vv),), inc=sS)
            vch(lambda e: e.tensor_scalar_add(sc8a[:, 1:2], sc8a[:, 1:2], 1e-8), waits=[(sS, t)])
            vch(lambda e: e.reciprocal(sc8a[:, 2:3], sc8a[:, 1:2]))
            vch(lambda e, hs=hs: e.tensor_single_scalar(dn_sb[0:NSP, hs], dirs_sb[0:NSP, hs],
                                                        sc8a[0:NSP, 2:3], ALU.mult))
            vch(lambda e, hs=hs: e.tensor_mul(sc8b[:, :], pos_sb[0:NSP, hs], pos_sb[0:NSP, hs]))
            vch(lambda e: e.tensor_reduce(sc8a[:, 0:1], sc8b[:, :], AX.X, ALU.add))
            vch(lambda e, hs=hs: e.tensor_mul(sc8b[:, :], pos_sb[0:NSP, hs], dn_sb[0:NSP, hs]))
            vv = vch(lambda e: e.tensor_reduce(sc8a[:, 1:2], sc8b[:, :], AX.X, ALU.add))
            t = em("scalar", lambda e, h=h: e.activation(rs2_all[0:NSP, h:h + 1], lsT_sb[0:NSP, h:h + 1],
                                                         AF.Exp, bias=0.0, scale=-2.0),
                   waits=((sQ, w_in0),), inc=sS)
            vch(lambda e, h=h: e.tensor_mul(sc8a[:, 3:4], sc8a[:, 0:1], rs2_all[0:NSP, h:h + 1]),
                waits=[(sS, t)])
            vch(lambda e, h=h: e.tensor_scalar_mul(bq_all[0:NSP, h:h + 1], sc8a[:, 3:4], -0.5))
            vch(lambda e, h=h: e.tensor_add(bk_all[0:NSP, h:h + 1], bq_all[0:NSP, h:h + 1],
                                            laT_sb[0:NSP, h:h + 1]))
            vch(lambda e, h=h: e.tensor_scalar_mul(bt_all[0:NSP, h:h + 1], sc8a[:, 1:2], -1.0))
            vch(lambda e, hs=hs: e.tensor_copy(pbf[:], pos_sb[0:NSP, hs]))
            vv = vch(lambda e, hs=hs: e.tensor_copy(dnbf[:], dn_sb[0:NSP, hs]))
            t = em("tensor", lambda e: e.transpose(ps_t8[:], pbf[:], id_sb[0:NSP, 0:NSP]),
                   waits=((sV, vv),), inc=sP)
            vv = em("vector", lambda e, h=h: e.tensor_copy(pdT_all[:, h * 16:h * 16 + 8], ps_t8[:]),
                    waits=((sP, t),), inc=sV)
            t = em("tensor", lambda e: e.transpose(ps_t8[:], dnbf[:], id_sb[0:NSP, 0:NSP]),
                   waits=((sV, vv),), inc=sP)
            em("vector", lambda e, h=h: e.tensor_copy(pdT_all[:, h * 16 + 8:h * 16 + 16], ps_t8[:]),
               waits=((sP, t),), inc=sV)

        t = em("scalar", lambda e: e.activation(sm01[0:1, 0:1], scal_sb[0:1, 2:3], AF.Sigmoid),
               waits=((sQ, w_in0),), inc=sS)
        vch(lambda e: e.tensor_scalar(sm01[0:1, 1:2], sm01[0:1, 0:1], 0.15, 0.03,
                                      ALU.mult, ALU.min), waits=[(sS, t)])
        vch(lambda e: e.tensor_scalar(sm01[0:1, 2:3], sm01[0:1, 1:2], -1.0, 1.0,
                                      ALU.mult, ALU.add))
        vv = vch(lambda e: e.tensor_scalar_mul(sm01[0:1, 3:4], scal_sb[0:1, 1:2], SQH))
        bcasts = ((sm01[0:1, 1:2], blcol, 128), (sm01[0:1, 2:3], ombcol, 128),
                  (sm01[0:1, 3:4], c0col, NSP), (scal_sb[0:1, 0:1], None, NSP))
        for (srcc, dst, prt) in bcasts:
            t = em("tensor", lambda e, srcc=srcc: e.matmul(ps_c[:], onesf[0:1, 0:128], srcc,
                                                           start=True, stop=True),
                   waits=((sV, vv),), inc=sP)
            if dst is not None:
                vv = vch(lambda e, dst=dst, prt=prt: e.tensor_copy(dst[0:prt, 0:1], ps_c[0:prt, 0:1]),
                         waits=[(sP, t)])
            else:
                vv = vch(lambda e: e.tensor_copy(gcol3[0:NSP, 3:4], ps_c[0:NSP, 0:1]),
                         waits=[(sP, t)])
        vch(lambda e: e.tensor_mul(c1col[:], c0col[:], gcol3[0:NSP, 3:4]))
    prologue_v = sV.n
    prologue_s = sS.n
    prologue_p = sP.n

    with nc.sbuf_tensor("xstage", [128, 2 * KC * TSTR], F32) as xstage, \
         nc.sbuf_tensor("xbb", [128, 2 * KC * TSTR], BF16) as xbb_t, \
         nc.sbuf_tensor("x2b", [128, 2 * KC * TSTR], BF16) as x2b_t, \
         nc.sbuf_tensor("abr", [1, 7 * TSTR], F32) as abr, \
         nc.sbuf_tensor("abc_sb", [128, 2 * 2 * TSTR], F32) as abc_sb, \
         nc.sbuf_tensor("lntmp", [128, 2 * TSTR], F32) as lntmp, \
         nc.psum_tensor([1, TSTR], F32) as ps_sx0, \
         nc.psum_tensor([1, TSTR], F32) as ps_sx1, \
         nc.psum_tensor([1, TSTR], F32) as ps_sx2, \
         nc.psum_tensor([1, TSTR], F32) as ps_sy0, \
         nc.psum_tensor([1, TSTR], F32) as ps_sy1, \
         nc.psum_tensor([1, TSTR], F32) as ps_sy2, \
         nc.psum_tensor([128, 2 * TSTR], F32) as ps_ab0, \
         nc.psum_tensor([128, 2 * TSTR], F32) as ps_ab1:
        ps_sx = [ps_sx0, ps_sx1, ps_sx2]
        ps_sy = [ps_sy0, ps_sy1, ps_sy2]
        ps_ab = [ps_ab0, ps_ab1]
        Tm, Te, Tv, Tr, Ta, Tb, Tt = [slice(i * TSTR, (i + 1) * TSTR) for i in range(7)]
        apply_done = []
        stats_read = []
        bcast_done = []
        evac_done = []
        cast_hist = []
        for s in range(NSTR):
            buf = s % 2
            b3 = s % 3
            xs = xstage[:, buf * KC * TSTR:(buf + 1) * KC * TSTR]
            xbb = xbb_t[:, buf * KC * TSTR:(buf + 1) * KC * TSTR]
            x2b = x2b_t[:, buf * KC * TSTR:(buf + 1) * KC * TSTR]
            if s >= 2:
                wts = [(sV, apply_done[s - 2])]
            else:
                wts = [(sV, prologue_v), (sS, prologue_s), (sP, prologue_p)]
            d = DMA("sync", xs, xTb[:, s * KC * TSTR:(s + 1) * KC * TSTR], sem=sX[buf], waits=wts)
            cw = [(sX[buf], d)]
            if s >= 2:
                cw.append((sP, bcast_done[s - 2]))
            if debug == 2:
                if s == 0:
                    DMA("sync", dbg_xs[:, :], xs, waits=[(sX[buf], d)])
            cast = em("vector", lambda e, xs=xs, xbb=xbb: e.tensor_copy(xbb[:], xs),
                      waits=cw, inc=sV)
            cast_hist.append(cast)
            sq = em("gpsimd", lambda e, xbb=xbb, x2b=x2b: e.tensor_mul(x2b[:], xbb[:], xbb[:]),
                    waits=[(sV, cast)], inc=sL)
            mm_w = [(sV, cast), (sL, sq), (sV, prologue_v), (sS, prologue_s)]
            if s >= 3:
                mm_w.append((sV, stats_read[s - 3]))
            for kc in range(KC):
                em("tensor", lambda e, kc=kc, b3=b3, xbb=xbb: e.matmul(
                    ps_sx[b3][:], onescol[:, 0:1], xbb[:, kc * TSTR:(kc + 1) * TSTR],
                    start=(kc == 0), stop=(kc == KC - 1)), waits=mm_w if kc == 0 else ())
            for kc in range(KC):
                em("tensor", lambda e, kc=kc, b3=b3, x2b=x2b: e.matmul(
                    ps_sy[b3][:], onescol[:, 0:1], x2b[:, kc * TSTR:(kc + 1) * TSTR],
                    start=(kc == 0), stop=(kc == KC - 1)),
                   inc=(sP if kc == KC - 1 else None))
            pmm = sP.n
            bcast_done.append(pmm)
            vch(lambda e, b3=b3: e.tensor_scalar_mul(abr[0:1, Tm], ps_sx[b3][:], 1.0 / DM),
                waits=[(sP, pmm)])
            vch(lambda e, b3=b3: e.tensor_scalar_mul(abr[0:1, Te], ps_sy[b3][:], 1.0 / DM))
            vch(lambda e: e.tensor_mul(abr[0:1, Tt], abr[0:1, Tm], abr[0:1, Tm]))
            vch(lambda e: e.tensor_sub(abr[0:1, Tv], abr[0:1, Te], abr[0:1, Tt]))
            stats_read.append(vch(lambda e: e.tensor_scalar_add(abr[0:1, Tv], abr[0:1, Tv], 1e-5)))
            t = em("scalar", lambda e: e.activation(abr[0:1, Tr], abr[0:1, Tv], AF.Sqrt),
                   waits=[(sV, sV.n)], inc=sS)
            vch(lambda e: e.reciprocal(abr[0:1, Ta], abr[0:1, Tr]), waits=[(sS, t)])
            vsm = vch(lambda e: e.scalar_tensor_tensor(abr[0:1, Tb], abr[0:1, Tm], -1.0, abr[0:1, Ta],
                                                       ALU.mult, ALU.mult))
            bw = [(sV, vsm), (sP, prologue_p)]
            if s >= 2:
                bw.append((sV, evac_done[s - 2]))
            em("tensor", lambda e, buf=buf: e.matmul(ps_ab[buf][:], onesf[0:1, 0:128],
                                                     abr[0:1, 4 * TSTR:6 * TSTR],
                                                     start=True, stop=True), waits=bw, inc=sP)
            abo = buf * 2 * TSTR
            aw = [(sP, sP.n)]
            em("vector", lambda e, buf=buf, abo=abo: e.tensor_copy(
                abc_sb[:, abo:abo + 2 * TSTR], ps_ab[buf][:]), waits=aw, inc=sV)
            evac_done.append(sV.n)
            vev = sV.n
            for kc in range(KC):
                lt = lntmp[:, (kc % 2) * TSTR:(kc % 2 + 1) * TSTR]
                em("vector", lambda e, kc=kc, xs=xs, abo=abo, lt=lt: e.tensor_mul(
                    lt[:], xs[:, kc * TSTR:(kc + 1) * TSTR], abc_sb[:, abo:abo + TSTR]),
                   waits=[(sV, sV.n)])
                em("vector", lambda e, kc=kc, s=s, abo=abo, lt=lt: e.tensor_add(
                    xnT[:, kc * S + s * TSTR:kc * S + (s + 1) * TSTR], lt[:],
                    abc_sb[:, abo + TSTR:abo + 2 * TSTR]),
                   waits=[(sV, sV.n)], inc=(sV if kc == KC - 1 else None))
            apply_done.append(sV.n)
            if debug == 2 and s == 0:
                DMA("sync", dbg_abr[0:1, 0:7 * TSTR], abr[:, :], waits=[(sL, sL.n)])
                DMA("sync", dbg_abc[:, 0:4 * TSTR], abc_sb[:, :])
        ln_done = sL.n
        ln_v = sV.n
        ln_s = sS.n

    emit_gamma_units(20)

    wstage = EA(nc.sbuf_tensor("wstage", [128, 2048], F32))
    wq_bf = EA(nc.sbuf_tensor("wq_bf", [128, 2048], BF16))
    wk_bf = EA(nc.sbuf_tensor("wk_bf", [128, 2048], BF16))
    wv_bf = EA(nc.sbuf_tensor("wv_bf", [128, 2048], BF16))
    qT = EA(nc.sbuf_tensor("qT", [128, S], BF16))
    kT = EA(nc.sbuf_tensor("kT", [128, S], BF16))
    vA = EA(nc.sbuf_tensor("vA", [128, 16 * 129], BF16))
    q2 = EA(nc.sbuf_tensor("q2", [128, S], BF16))
    qsq_sb = EA(nc.sbuf_tensor("qsq_sb", [1, S], BF16))
    Aq = EA(nc.sbuf_tensor("Aq", [16, S], BF16))
    Bk = EA(nc.sbuf_tensor("Bk", [16, S], BF16))
    d_t = EA(nc.sbuf_tensor("d_t", [NSP, S], BF16))
    a1buf = EA(nc.sbuf_tensor("a1buf", [NSP, S], BF16))
    expb = EA(nc.sbuf_tensor("expb", [128, 2 * 512], BF16))
    o_std = EA(nc.sbuf_tensor("o_std", [128, 16 * 128], BF16))
    obl = EA(nc.sbuf_tensor("obl", [128, 2 * 128], BF16))
    gcolB = EA(nc.sbuf_tensor("gcolB", [128, 8], F32))
    wo_bf = EA(nc.sbuf_tensor("wo_bf", [128, H * 2048], BF16))

    ln_guard = [(sL, ln_done), (sV, ln_v), (sS, ln_s), (sP, sP.n)]
    em("vector", lambda e: e.memset(vA[:], 1.0), waits=ln_guard, inc=sV)

    proj_done = {}

    def weight_chain(h, first):
        wts = list(ln_guard) if first else [(sP, proj_done[h - 1])]
        d1 = DMA("sync", wstage[:, :], wq[h, :, :], sem=sW, waits=wts)
        c1 = em("vector", lambda e: e.tensor_copy(wq_bf[:], wstage[:, :]), waits=[(sW, d1)], inc=sV)
        d2 = DMA("sync", wstage[:, :], wk[h, :, :], sem=sW, waits=[(sV, c1)])
        c2 = em("vector", lambda e: e.tensor_copy(wk_bf[:], wstage[:, :]), waits=[(sW, d2)], inc=sV)
        d3 = DMA("sync", wstage[:, :], wv[h, :, :], sem=sW, waits=[(sV, c2)])
        c3 = em("vector", lambda e: e.tensor_copy(wv_bf[:], wstage[:, :]), waits=[(sW, d3)], inc=sV)
        return {"v": c3}

    wc = weight_chain(0, True)
    wo_ready = None
    prev_exp_bank = {}
    prev_av_bank = {}
    head_end_s = ln_s
    wpieces = []
    wchain_last = [None]

    def emit_wpiece():
        if wpieces:
            p = wpieces.pop(0)
            if p is not None:
                p()

    for h in range(H):
        with nc.psum_tensor([128, 512], F32) as ps_p0, \
             nc.psum_tensor([128, 512], F32) as ps_p1:
            ps_p = [ps_p0, ps_p1]
            evacs = {}
            base_w = [(sV, wc["v"]), (sL, ln_done), (sV, sV.n), (sS, head_end_s)]
            for (name, wbf, dst) in (("q", wq_bf, qT), ("k", wk_bf, kT)):
                for tc in range(4):
                    buf = tc % 2
                    mw = list(base_w) if (name == "q" and tc == 0) else []
                    prevkey = (name, tc - 2) if tc >= 2 else (("q", tc + 2) if name == "k" else None)
                    if prevkey in evacs:
                        mw.append((sV, evacs[prevkey]))
                    for kc in range(KC):
                        em("tensor", lambda e, kc=kc, tc=tc, buf=buf, wbf=wbf: e.matmul(
                            ps_p[buf][:], wbf[:, kc * 128:(kc + 1) * 128],
                            xnT[:, kc * S + tc * 512:kc * S + tc * 512 + 512],
                            start=(kc == 0), stop=(kc == KC - 1)),
                           waits=mw if kc == 0 else (), inc=(sP if kc == KC - 1 else None))
                    pmm = sP.n
                    evacs[(name, tc)] = em(
                        "vector", lambda e, tc=tc, buf=buf, dst=dst: e.tensor_copy(
                            dst[:, tc * 512:(tc + 1) * 512], ps_p[buf][:]),
                        waits=[(sP, pmm)], inc=sV)
            for tt in range(16):
                buf = tt % 2
                mw = []
                prevkey = ("v", tt - 2) if tt >= 2 else ("k", 2 + tt)
                if prevkey in evacs:
                    mw.append((sV, evacs[prevkey]))
                for kc in range(KC):
                    em("tensor", lambda e, kc=kc, tt=tt, buf=buf: e.matmul(
                        ps_p[buf][:, 0:128], xnT[:, kc * S + tt * 128:kc * S + (tt + 1) * 128],
                        wv_bf[:, kc * 128:(kc + 1) * 128],
                        start=(kc == 0), stop=(kc == KC - 1)),
                       waits=mw if kc == 0 else (), inc=(sP if kc == KC - 1 else None))
                pmm = sP.n
                evacs[("v", tt)] = em(
                    "vector", lambda e, tt=tt, buf=buf: e.tensor_copy(
                        vA[:, tt * 129:tt * 129 + 128], ps_p[buf][:, 0:128]),
                    waits=[(sP, pmm)], inc=sV)
            proj_v = sV.n
        proj_done[h] = sP.n

        wchain_last[0] = None

        def _wp(src_ap, dst, h=h):
            def go():
                prev = wchain_last[0]
                wts = [(sP, proj_done[h])] if prev is None else [(sV, prev)]
                d = DMA("sync", wstage[:, :], src_ap, sem=sW, waits=wts)
                wchain_last[0] = em("vector", lambda e, dst=dst: e.tensor_copy(dst, wstage[:, :]),
                                    waits=[(sW, d)], inc=sV)
            return go

        if h + 1 < H:
            wpieces = [None, None,
                       _wp(wq[h + 1, :, :], wq_bf[:]),
                       _wp(wk[h + 1, :, :], wk_bf[:]),
                       _wp(wv[h + 1, :, :], wv_bf[:])]
        else:
            wpieces = [_wp(wo[hh, :, :], wo_bf[:, hh * 2048:(hh + 1) * 2048])
                       for hh in range(H)]

        with nc.psum_tensor([128, 512], F32) as ps_sA0, \
             nc.psum_tensor([128, 512], F32) as ps_sA1, \
             nc.psum_tensor([128, 129], F32) as ps_oA0, \
             nc.psum_tensor([128, 129], F32) as ps_oA1, \
             nc.psum_tensor([128, 129], F32) as ps_oA2, \
             nc.psum_tensor([128, 129], F32) as ps_oA3, \
             nc.psum_tensor([1, 512], F32) as ps_q10, \
             nc.psum_tensor([40, 512], F32) as ps_gd0:
            ps_s = [ps_sA0, ps_sA1]
            ps_o = [ps_oA0, ps_oA1, ps_oA2, ps_oA3]

            def emit_side(nm, src, dstf, ball, is_q):
                sq2 = em("vector", lambda e, src=src: e.tensor_mul(q2[:], src[:], src[:]),
                         waits=[(sV, proj_v), (sP, sP.n)], inc=sV)
                qev = {}
                for tc in range(4):
                    mw = [(sV, sq2), (sS, sS.n), (sV, sV.n)]
                    if tc >= 1:
                        mw.append((sV, qev[tc - 1]))
                    em("tensor", lambda e, tc=tc: e.matmul(
                        ps_q10[:], onescol[:, 0:1], q2[:, tc * 512:(tc + 1) * 512],
                        start=True, stop=True), waits=mw, inc=sP)
                    pmm = sP.n
                    qev[tc] = em("vector", lambda e, tc=tc: e.tensor_copy(
                        qsq_sb[0:1, tc * 512:(tc + 1) * 512], ps_q10[:]),
                        waits=[(sP, pmm)], inc=sV)
                gd_ev = {}
                for tc in range(4):
                    mw = [(sV, qev[tc])]
                    if tc >= 1:
                        mw.append((sS, gd_ev[tc - 1]))
                    em("tensor", lambda e, tc=tc, src=src, h=h: e.matmul(
                        ps_gd0[0:NSP, :], pdT_all[:, h * 16:h * 16 + 8],
                        src[:, tc * 512:(tc + 1) * 512], start=True, stop=False), waits=mw)
                    em("tensor", lambda e, tc=tc: e.matmul(
                        ps_gd0[0:NSP, :], c8[0:1, 0:NSP],
                        qsq_sb[0:1, tc * 512:(tc + 1) * 512], start=False, stop=True))
                    em("tensor", lambda e, tc=tc, src=src, h=h: e.matmul(
                        ps_gd0[32:40, :], pdT_all[:, h * 16 + 8:h * 16 + 16],
                        src[:, tc * 512:(tc + 1) * 512], start=True, stop=True,
                        tile_position=(0, 32)), inc=sP)
                    pmm = sP.n
                    em("scalar", lambda e, tc=tc, dstf=dstf, ball=ball, h=h: e.activation(
                        dstf[0:NSP, tc * 512:(tc + 1) * 512], ps_gd0[0:NSP, :], AF.Exp,
                        bias=ball[0:NSP, h:h + 1], scale=rs2_all[0:NSP, h:h + 1]),
                        waits=[(sP, pmm)])
                    gd_ev[tc] = em("scalar", lambda e, tc=tc, h=h: e.activation(
                        d_t[0:NSP, tc * 512:(tc + 1) * 512], ps_gd0[32:40, :], AF.Tanh,
                        bias=bt_all[0:NSP, h:h + 1], scale=ones8[0:NSP, 0:1]), inc=sS)
                scev = sS.n
                if is_q:
                    a1 = vch(lambda e: e.scalar_tensor_tensor(
                        a1buf[0:NSP, :], Aq[0:NSP, :], c1col[0:NSP, 0:1], d_t[0:NSP, :],
                        ALU.mult, ALU.mult), waits=[(sV, prologue_v), (sS, scev), (sG, sG.n)])
                    vch(lambda e: e.tensor_single_scalar(
                        Aq[0:NSP, :], Aq[0:NSP, :], c0col[0:NSP, 0:1], ALU.mult))
                else:
                    a1 = vch(lambda e: e.tensor_mul(a1buf[0:NSP, :], Bk[0:NSP, :], d_t[0:NSP, :]),
                             waits=[(sS, scev), (sG, sG.n)])
                em("gpsimd", lambda e, dstf=dstf: e.dma_start(
                    out=dstf[NSP:16, :], in_=a1buf[0:NSP, :]), waits=[(sV, a1)], inc=sG, amt=16)

            def do_av(ii, nkc, exp_t, mask_t, evac_v, first_w=()):
                bb = ii % 2
                aw = [(sL, mask_t[ii])] if ii in mask_t else [(sS, exp_t[ii])]
                if ii == 0:
                    aw.append((sV, evac_v))
                    aw += list(first_w)
                for sq_i in range(4):
                    em("tensor", lambda e, bb=bb, ii=ii, sq_i=sq_i, nkc=nkc: e.matmul(
                        ps_o[sq_i][:], expb[:, bb * 512 + sq_i * 128:bb * 512 + (sq_i + 1) * 128],
                        vA[:, ii * 129:(ii + 1) * 129],
                        start=(ii == 0), stop=(ii == nkc - 1)),
                       waits=aw if sq_i == 0 else (), inc=(sP if sq_i == 3 else None))
                prev_av_bank[bb] = sP.n
                return sP.n

            def run_cells(qc, br, base_mw, first_w, pending=()):
                nkc = 4 * (qc + 1)
                qs = slice(qc * 512, (qc + 1) * 512)
                exp_t = {}
                mask_t = {}
                evac_v = sV.n
                for i in range(nkc):
                    if i == 2:
                        for f in pending:
                            f()
                    bank = i % 2
                    ks = slice(i * 128, (i + 1) * 128)
                    mw = list(base_mw)
                    if bank in prev_exp_bank:
                        mw.append((sS, prev_exp_bank[bank]))
                    if br == 0:
                        em("tensor", lambda e, bank=bank, ks=ks, qs=qs: e.matmul(
                            ps_s[bank][:], kT[:, ks], qT[:, qs], start=True, stop=True),
                           waits=mw, inc=sP)
                    else:
                        em("tensor", lambda e, bank=bank, ks=ks, qs=qs: e.matmul(
                            ps_s[bank][:], kT[:, ks], qT[:, qs], start=True, stop=False),
                           waits=mw)
                        em("tensor", lambda e, bank=bank, ks=ks, qs=qs: e.matmul(
                            ps_s[bank][:], Bk[0:16, ks], Aq[0:16, qs], start=False, stop=True),
                           inc=sP)
                    sc_t = sP.n
                    if i >= 1:
                        do_av(i - 1, nkc, exp_t, mask_t, evac_v, first_w)
                    ew = [(sP, sc_t)]
                    if bank in prev_av_bank:
                        ew.append((sP, prev_av_bank[bank]))
                    exp_t[i] = em("scalar", lambda e, bank=bank: e.activation(
                        expb[:, bank * 512:(bank + 1) * 512], ps_s[bank][:], AF.Exp,
                        bias=0.0, scale=ISQ), waits=ew, inc=sS)
                    prev_exp_bank[bank] = exp_t[i]
                    if i >= 4 * qc:
                        base = qc * 512 - i * 128
                        mask_t[i] = em("gpsimd", lambda e, bank=bank, base=base: e.affine_select(
                            expb[:, bank * 512:(bank + 1) * 512],
                            expb[:, bank * 512:(bank + 1) * 512],
                            [[1, 512]], ALU.is_ge, 0.0, base=base, channel_multiplier=-1),
                            waits=[(sS, exp_t[i])], inc=sL)
                return do_av(nkc - 1, nkc, exp_t, mask_t, evac_v, first_w)

            for qc in range(4):
                av_last = run_cells(qc, 0, [(sV, proj_v)], ())
                for sq_i in range(4):
                    oc = qc * 4 + sq_i
                    vch(lambda e, sq_i=sq_i, oc=oc: e.reciprocal(
                        gcolB[:, 0:1], ps_o[sq_i][:, 128:129]),
                        waits=[(sP, av_last)])
                    vch(lambda e: e.tensor_mul(
                        gcolB[:, 0:1], gcolB[:, 0:1], ombcol[:, 0:1]))
                    vch(lambda e, sq_i=sq_i, oc=oc: e.tensor_single_scalar(
                        o_std[:, oc * 128:(oc + 1) * 128], ps_o[sq_i][:, 0:128],
                        gcolB[:, 0:1], ALU.mult))
                if qc == 0:
                    emit_side("q", qT, Aq, bq_all, True)
                elif qc == 1:
                    emit_side("k", kT, Bk, bk_all, False)
                emit_wpiece()
            splat_v = sV.n
            splat_g = sG.n
            splat_s = sS.n

        with nc.psum_tensor([128, 512], F32) as ps_sB0, \
             nc.psum_tensor([128, 512], F32) as ps_sB1, \
             nc.psum_tensor([128, 129], F32) as ps_oB0, \
             nc.psum_tensor([128, 129], F32) as ps_oB1, \
             nc.psum_tensor([128, 129], F32) as ps_oB2, \
             nc.psum_tensor([128, 129], F32) as ps_oB3, \
             nc.psum_tensor([128, 128], BF16) as ps_tr0, \
             nc.psum_tensor([128, 128], BF16) as ps_tr1:
            ps_s = [ps_sB0, ps_sB1]
            ps_o = [ps_oB0, ps_oB1, ps_oB2, ps_oB3]
            ps_tr = [ps_tr0, ps_tr1]
            tr_ev = {}
            bmw = [(sV, splat_v), (sG, splat_g), (sS, splat_s)]
            pend = []
            for qc in range(4):
                av_last = run_cells(qc, 1, bmw, [(sS, splat_s)], pending=pend)
                pend = []
                for sq_i in range(4):
                    oc = qc * 4 + sq_i
                    slot = sq_i % 2
                    vch(lambda e, sq_i=sq_i: e.reciprocal(
                        gcolB[:, 4:5], ps_o[sq_i][:, 128:129]),
                        waits=[(sP, av_last)])
                    vch(lambda e: e.tensor_mul(
                        gcolB[:, 4:5], gcolB[:, 4:5], blcol[:, 0:1]))
                    ob = vch(lambda e, sq_i=sq_i, oc=oc, slot=slot: e.scalar_tensor_tensor(
                        obl[:, slot * 128:(slot + 1) * 128], ps_o[sq_i][:, 0:128],
                        gcolB[:, 4:5], o_std[:, oc * 128:(oc + 1) * 128],
                        ALU.mult, ALU.add))

                    def mk_tr(oc=oc, slot=slot, ob=ob, h=h):
                        def go():
                            tw = [(sV, ob)]
                            prev = tr_ev.get(slot)
                            if prev is not None:
                                tw.append((sV, prev))
                            tt_g = em("tensor", lambda e, slot=slot: e.transpose(
                                ps_tr[slot][:], obl[:, slot * 128:(slot + 1) * 128],
                                id_sb[:, :]), waits=tw, inc=sP)
                            tr_ev[slot] = em(
                                "vector", lambda e, oc=oc, h=h, slot=slot: e.tensor_copy(
                                    oT[:, h * S + oc * 128:h * S + (oc + 1) * 128],
                                    ps_tr[slot][:]), waits=[(sP, tt_g)], inc=sV)
                        return go
                    pend.append(mk_tr())
                emit_wpiece()
            for f in pend:
                f()
        wc = {"v": wchain_last[0]}
        if h + 1 == H:
            wo_ready = wchain_last[0]
        head_end_s = sS.n
        emit_gamma_units(14)

    emit_gamma_units(NGU)
    heads_v = sV.n
    heads_s = sS.n

    vch(lambda e: e.tensor_reduce(gcol3[:, 0:1], gacc[:, 0:2 * FC], AX.X, ALU.add),
        waits=[(sS, gamma_state["sc"][-1])])
    vch(lambda e: e.tensor_reduce(gcol3[:, 1:2], gacc[:, 2 * FC:4 * FC], AX.X, ALU.add))
    gv = vch(lambda e: e.tensor_reduce(gcol3[:, 2:3], gacc[:, 4 * FC:6 * FC], AX.X, ALU.add))
    gl = em("gpsimd", lambda e: e.tensor_reduce(grow3[0:1, 0:3], gcol3[:, 0:3], AX.C, ALU.add),
            waits=[(sV, gv)], inc=sL)
    gd = em("gpsimd", lambda e: e.dma_start(out=gam_in[:, :], in_=grow3[0:1, 0:3]),
            waits=[(sL, gl)], inc=sG, amt=16)
    em("gpsimd", lambda e: e.collective_compute(
        "AllReduce", ALU.add, replica_groups=ALL8,
        ins=[gam_in.ap().opt()], outs=[gam_out.ap().opt()]), waits=[(sG, gd)], inc=sC)
    gamma_coll = sC.n

    with nc.Block() as block:
        bld.flush(block)

    with nc.sbuf_tensor("attn_st", [128, 2 * 512], F32) as attn_st, \
         nc.psum_tensor("ps_w0", [128, 512], F32) as ps_w0, \
         nc.psum_tensor("ps_w1", [128, 512], F32) as ps_w1:
        ps_w = [ps_w0, ps_w1]
        ev = {}
        st_hist = {}
        last_store = None
        idx = 0
        for tt in range(16):
            for dc in range(4):
                buf = idx % 2
                mw = [(sV, wo_ready), (sV, heads_v), (sS, heads_s)]
                if idx >= 2:
                    mw.append((sV, ev[idx - 2]))
                for hh in range(H):
                    em("tensor", lambda e, hh=hh, tt=tt, dc=dc, buf=buf: e.matmul(
                        ps_w[buf][:], oT[:, hh * S + tt * 128:hh * S + (tt + 1) * 128],
                        wo_bf[:, hh * 2048 + dc * 512:hh * 2048 + (dc + 1) * 512],
                        start=(hh == 0), stop=(hh == H - 1)),
                       waits=mw if hh == 0 else (), inc=(sP if hh == H - 1 else None))
                pmm = sP.n
                evw = [(sP, pmm)]
                if idx >= 2:
                    evw.append(st_hist[idx - 2])
                ev[idx] = em("vector", lambda e, buf=buf: e.tensor_copy(
                    attn_st[:, buf * 512:(buf + 1) * 512], ps_w[buf][:]),
                    waits=evw, inc=sV)
                last_store = DMA("sync", part_secs[0][tt * 128:(tt + 1) * 128, dc * 512:(dc + 1) * 512],
                                 attn_st[:, buf * 512:(buf + 1) * 512], sem=sA[buf],
                                 waits=[(sV, ev[idx])])
                st_hist[idx] = (sA[buf], last_store)
                idx += 1
        em("gpsimd", lambda e: e.collective_compute(
            "ReduceScatter", ALU.add, replica_groups=PAIRS,
            ins=[part_secs[0].ap().opt()], outs=[rs_secs[0].ap().opt()]),
           waits=[(sA[0], sA[0].n), (sA[1], sA[1].n)], inc=sC)
        attn_coll = sC.n

        if debug == 2:
            DMA("sync", dbg_xn[:, :], xnT[:, 0:2048], waits=[(sC, rs1_coll)])
            DMA("sync", dbg_q[:, :], qT[:, :])
            DMA("sync", dbg_k[:, :], kT[:, :])
            DMA("sync", dbg_v[:, :], vA[:, :])
            DMA("sync", dbg_aq[:, :], Aq[:, :])
            DMA("sync", dbg_bk[:, :], Bk[:, :])
            DMA("sync", dbg_qsq[:, :], qsq_sb[:, :])
            DMA("sync", dbg_ot[:, :], oT[:, :])
            DMA("sync", dbg_expb[:, :], expb[:, :])
            DMA("sync", dbg_gb[:, :], gcolB[:, :])
            DMA("sync", dbg_ostd[:, :], o_std[:, 0:512])
            DMA("sync", dbg_pdt[:, :], pdT_all[:, :])
        with nc.Block() as block:
            bld.flush(block)
    esA.close()

    esC = ExitStack()
    EC = esC.enter_context
    hid = EC(nc.sbuf_tensor("hid", [128, FCG * S], BF16))
    w12stg = EC(nc.sbuf_tensor("w12stg", [128, 2 * 2048], F32))
    w1q = EC(nc.sbuf_tensor("w1q", [128, 2 * 2048], BF16))
    w2q = EC(nc.sbuf_tensor("w2q", [128, 2 * 2048], BF16))
    tmpneg = EC(nc.sbuf_tensor("tmpneg", [128, 1024], F32))
    thcol = EC(nc.sbuf_tensor("thcol", [128, 4], F32))
    nthcol = EC(nc.sbuf_tensor("nthcol", [128, 4], F32))
    sig_sb = EC(nc.sbuf_tensor("sig_sb", [128, 2 * 512], BF16))
    w3stg = EC(nc.sbuf_tensor("w3stg", [128, FCG * 512], F32))
    w3q = EC(nc.sbuf_tensor("w3q", [128, 2 * FCG * 512], BF16))
    fst = EC(nc.sbuf_tensor("fst", [128, 2 * 512], F32))
    th3 = EC(nc.sbuf_tensor("th3", [1, 8], F32))
    gamo_sb = EC(nc.sbuf_tensor("gamo_sb", [1, 4], F32))
    fin_f = EC(nc.sbuf_tensor("fin_f", [128, 2 * 2048], F32))
    fin_o = EC(nc.sbuf_tensor("fin_o", [128, 2 * 2048], F32))

    wo_guard = [(sP, sP.n), (sV, sV.n), (sS, sS.n), (sQ, sQ.n), (sL, sL.n)]
    gdr = em("scalar", lambda e: e.dma_start(out=gamo_sb[0:1, 0:3], in_=gam_out[:, :]),
             waits=[(sC, gamma_coll)] + wo_guard, inc=sSD, amt=16)
    vch(lambda e: e.tensor_scalar(th3[0:1, 0:3], gamo_sb[0:1, 0:3],
                                  1.0 / (8.0 * NW_ELEM), 5e-9, ALU.mult, ALU.add),
        waits=[(sSD, gdr)])
    thv = vch(lambda e: e.tensor_scalar_mul(th3[0:1, 4:7], th3[0:1, 0:3], -1.0))
    with nc.psum_tensor([128, 1], F32) as ps_th:
        for m in range(3):
            t = em("tensor", lambda e, m=m: e.matmul(ps_th[:, 0:1], onesf[0:1, 0:128],
                                                     th3[0:1, m:m + 1], start=True, stop=True),
                   waits=[(sV, thv), (sV, sV.n), (sS, sS.n)], inc=sP)
            vch(lambda e, m=m: e.tensor_copy(thcol[:, m:m + 1], ps_th[:, 0:1]),
                waits=[(sP, t)])
    th_v = vch(lambda e: e.tensor_scalar_mul(nthcol[:, 0:3], thcol[:, 0:3], -1.0))

    def quantize(dst, stg_ap, m, slot, waits, dst_war=None):
        qv = None
        for hh in range(2):
            w0 = list(waits) + [(sV, th_v), (sV, sV.n)] if hh == 0 else [(sV, sV.n)]
            pa = em("vector", lambda e, stg_ap=stg_ap, m=m, hh=hh: e.tensor_single_scalar(
                tmpneg[:, 0:1024], stg_ap[:, hh * 1024:(hh + 1) * 1024],
                nthcol[0:128, m:m + 1], ALU.is_le), waits=w0, inc=sV)
            qw = [(sV, pa)]
            if dst_war is not None and hh == 0:
                qw.append((sP, dst_war))
            qv = em("vector", lambda e, dst=dst, stg_ap=stg_ap, m=m, hh=hh: e.scalar_tensor_tensor(
                dst[:, hh * 1024:(hh + 1) * 1024], stg_ap[:, hh * 1024:(hh + 1) * 1024],
                thcol[:, m:m + 1], tmpneg[:, 0:1024],
                ALU.is_ge, ALU.subtract),
                waits=qw, inc=sV)
        return qv

    prev_grp_qv = None
    prev_w3_qv = None
    fdma = {}
    goff_list = [0, 8, 15]
    w3off_list = [0, 8 * 4 * 512, 15 * 4 * 512]
    for g in range(NGRP):
        gsz = GRPS[g]
        goff = goff_list[g]
        with nc.psum_tensor([128, 512], F32) as ps_xa0, \
             nc.psum_tensor([128, 512], F32) as ps_xa1, \
             nc.psum_tensor([128, 512], F32) as ps_xa2, \
             nc.psum_tensor([128, 512], F32) as ps_xb0, \
             nc.psum_tensor([128, 512], F32) as ps_xb1, \
             nc.psum_tensor([128, 512], F32) as ps_xb2:
            ps_xa = [ps_xa0, ps_xa1, ps_xa2]
            ps_xb = [ps_xb0, ps_xb1, ps_xb2]
            q_prev = {}
            q_prev2 = {}
            hid_v = {}
            slot_mm = {}
            sg_prev = None
            for fi in range(gsz):
                fc = goff + fi
                slot = fi % 2
                if fi >= 1:
                    wts1 = [(sV, q_prev[fi - 1])]
                elif g == 0:
                    wts1 = list(wo_guard)
                else:
                    wts1 = [(sV, prev_grp_qv)]
                d1 = DMA("sync", w12stg[:, 0:2048], w1[fc, :, :], sem=sW1, waits=wts1)
                qv1 = quantize(w1q[:, slot * 2048:(slot + 1) * 2048],
                               w12stg[:, 0:2048], 0, 0, [(sW1, d1)],
                               dst_war=slot_mm.get(slot))
                d2 = DMA("sync", w12stg[:, 2048:4096], w2[fc, :, :], sem=sW2,
                         waits=([(sV, q_prev2[fi - 1])] if fi >= 1 else
                                ([(sV, prev_grp_qv)] if g > 0 else [])))
                qv2 = quantize(w2q[:, slot * 2048:(slot + 1) * 2048],
                               w12stg[:, 2048:4096], 1, 0, [(sW2, d2)],
                               dst_war=slot_mm.get(slot))
                q_prev[fi] = qv1
                q_prev2[fi] = qv2
                for tc in range(4):
                    buf = (fi * 4 + tc) % 3
                    mw = [(sV, qv1), (sV, qv2)]
                    if safe >= 2:
                        mw += allw()
                    gi = fi * 4 + tc - 3
                    prev = hid_v.get((gi // 4, gi % 4)) if gi >= 0 else None
                    if prev is not None:
                        mw.append((sV, prev))
                    if g > 0 and fi == 0 and tc < 2:
                        mw.append((sV, sV.n))
                    for kc in range(KC):
                        em("tensor", lambda e, kc=kc, tc=tc, buf=buf, slot=slot: e.matmul(
                            ps_xa[buf][:], w1q[:, slot * 2048 + kc * 128:slot * 2048 + (kc + 1) * 128],
                            xnT[:, kc * S + tc * 512:kc * S + tc * 512 + 512],
                            start=(kc == 0), stop=(kc == KC - 1)),
                           waits=mw if kc == 0 else ())
                    for kc in range(KC):
                        em("tensor", lambda e, kc=kc, tc=tc, buf=buf, slot=slot: e.matmul(
                            ps_xb[buf][:], w2q[:, slot * 2048 + kc * 128:slot * 2048 + (kc + 1) * 128],
                            xnT[:, kc * S + tc * 512:kc * S + tc * 512 + 512],
                            start=(kc == 0), stop=(kc == KC - 1)),
                           inc=(sP if kc == KC - 1 else None))
                    pmm = sP.n
                    ss = tc % 2
                    sgw = [(sP, pmm)]
                    if sg_prev is not None:
                        sgw.append((sV, sg_prev))
                    if safe >= 2:
                        sgw += allw()
                    sg = em("scalar", lambda e, buf=buf, ss=ss: e.activation(
                        sig_sb[:, ss * 512:(ss + 1) * 512], ps_xa[buf][:], AF.Sigmoid),
                        waits=sgw, inc=sS)
                    hid_v[(fi, tc)] = em("vector", lambda e, fi=fi, tc=tc, buf=buf, ss=ss: e.tensor_mul(
                        hid[:, fi * S + tc * 512:fi * S + (tc + 1) * 512], ps_xb[buf][:],
                        sig_sb[:, ss * 512:(ss + 1) * 512]),
                        waits=[(sS, sg)], inc=sV)
                    gi2 = fi * 4 + tc - 2
                    sg_prev = hid_v.get((gi2 // 4, gi2 % 4)) if gi2 >= 0 else None
                slot_mm[slot] = sP.n
            prev_grp_qv = q_prev2[gsz - 1]
            hid_done = sV.n

        with nc.psum_tensor([128, 512], F32) as ps_f0, \
             nc.psum_tensor([128, 512], F32) as ps_f1:
            ps_f = [ps_f0, ps_f1]
            w3mm_prev = {}
            for dc in range(4):
                qslot = dc % 2
                qoff = qslot * FCG * 512
                w3qv = {}
                wts = [(sV, hid_done)]
                if prev_w3_qv is not None:
                    wts.append((sV, prev_w3_qv))
                woff = w3off_list[g] + dc * gsz * 512
                d = DMA("sync", w3stg[:, 0:gsz * 512], w3[:, woff:woff + gsz * 512],
                        sem=sW3[0], waits=wts)
                for fi in range(gsz):
                    pw = [(sW3[0], d)]
                    stw = []
                    if w3mm_prev.get(qslot) is not None:
                        stw.append((sP, w3mm_prev[qslot]))
                    if safe >= 1:
                        stw += allw()
                    pa = em("vector", lambda e, fi=fi: e.tensor_single_scalar(
                        tmpneg[:, 0:512], w3stg[:, fi * 512:(fi + 1) * 512],
                        nthcol[0:128, 2:3], ALU.is_le), waits=pw + [(sV, sV.n)], inc=sV)
                    w3qv[fi] = em("vector", lambda e, fi=fi, qoff=qoff: e.scalar_tensor_tensor(
                        w3q[:, qoff + fi * 512:qoff + (fi + 1) * 512],
                        w3stg[:, fi * 512:(fi + 1) * 512],
                        thcol[:, 2:3], tmpneg[:, 0:512], ALU.is_ge, ALU.subtract),
                        waits=stw + [(sV, pa)], inc=sV)
                w3q_ready = sV.n
                prev_w3_qv = w3qv[gsz - 1]
                ev = {}
                dlast = None
                last_grp = (g == NGRP - 1)
                tt_order = ([t for q in range(4) for t in (2 * q, 2 * q + 8, 2 * q + 1, 2 * q + 9)]
                            if last_grp else list(range(16)))
                for idx, tt in enumerate(tt_order):
                    buf = idx % 2
                    mw = [(sV, w3q_ready)]
                    if idx >= 2:
                        mw.append((sV, ev[idx - 2]))
                    if safe >= 1:
                        mw += allw()
                    for fi in range(gsz):
                        em("tensor", lambda e, fi=fi, tt=tt, buf=buf, gsz=gsz, qoff=qoff: e.matmul(
                            ps_f[buf][:], hid[:, fi * S + tt * 128:fi * S + (tt + 1) * 128],
                            w3q[:, qoff + fi * 512:qoff + (fi + 1) * 512],
                            start=(fi == 0), stop=(fi == gsz - 1)),
                           waits=mw if fi == 0 else (), inc=(sP if fi == gsz - 1 else None))
                    pmm = sP.n
                    evw = [(sP, pmm)]
                    prevst = fdma.get(idx - 2 if idx >= 2 else 14 + idx)
                    if prevst is not None:
                        evw.append(prevst)
                    if safe >= 1:
                        evw += allw()
                    ev[idx] = em("vector", lambda e, buf=buf: e.tensor_copy(
                        fst[:, buf * 512:(buf + 1) * 512], ps_f[buf][:]), waits=evw, inc=sV)
                    if last_grp:
                        row0 = ((tt % 8) // 2) * 512 + (tt // 8) * 256 + (tt % 2) * 128
                    else:
                        row0 = tt * 128
                    dlast = DMA("sync",
                                part_secs[1 + g][row0:row0 + 128, dc * 512:(dc + 1) * 512],
                                fst[:, buf * 512:(buf + 1) * 512], sem=sF[buf],
                                waits=[(sV, ev[idx])])
                    fdma[idx] = (sF[buf], dlast)
                    if dc == 3 and last_grp and idx % 4 == 3:
                        qq = idx // 4
                        em("gpsimd", lambda e, g=g, qq=qq: e.collective_compute(
                            "ReduceScatter", ALU.add, replica_groups=PAIRS,
                            ins=[part_secs[1 + g][qq * 512:(qq + 1) * 512, :].opt()],
                            outs=[rs_secs[1 + g][qq * 256:(qq + 1) * 256, :].opt()]),
                           waits=[(sF[0], sF[0].n), (sF[1], sF[1].n)], inc=sC)
                w3mm_prev[qslot] = sP.n
                if dc == 3 and g < NGRP - 1:
                    em("gpsimd", lambda e, g=g: e.collective_compute(
                        "ReduceScatter", ALU.add, replica_groups=PAIRS,
                        ins=[part_secs[1 + g].ap().opt()], outs=[rs_secs[1 + g].ap().opt()]),
                       waits=[(sF[0], sF[0].n), (sF[1], sF[1].n)], inc=sC)

    rs_coll = sC.n

    n_out = 0
    prev_adds = {}
    for tt in range(8):
        osl = tt % 2
        dxw = [(sO, 16 * (tt - 1))] if tt >= 2 else []
        dx = DMA("sync", fin_o[:, osl * 2048:(osl + 1) * 2048],
                 xOwn[tt * 128:(tt + 1) * 128, :], waits=dxw)
        vv = None
        for j in range(1 + NGRP):
            fs = j % 2
            if j == NGRP:
                fw = [(sC, attn_coll + NGRP - 1 + tt // 2 + 1)]
            else:
                fw = [(sC, attn_coll + j)]
            pa = prev_adds.get(fs)
            if pa is not None:
                fw.append((sV, pa))
            df = DMA("sync", fin_f[:, fs * 2048:(fs + 1) * 2048],
                     rs_secs[j][tt * 128:(tt + 1) * 128, :],
                     sem=sX[fs], waits=fw)
            vv = em("vector", lambda e, fs=fs, osl=osl: e.tensor_add(
                fin_o[:, osl * 2048:(osl + 1) * 2048], fin_o[:, osl * 2048:(osl + 1) * 2048],
                fin_f[:, fs * 2048:(fs + 1) * 2048]),
                waits=[(sX[fs], df), (sQ, dx), (sO, 16 * (tt - 1)), (sV, sV.n)], inc=sV)
            prev_adds[fs] = vv
        em("sync", lambda e, tt=tt, osl=osl: e.dma_start(
            out=out[tt * 128:(tt + 1) * 128, :], in_=fin_o[:, osl * 2048:(osl + 1) * 2048]),
           waits=[(sV, vv)], inc=sO, amt=16)
        n_out += 1

    if debug:
        em("vector", lambda e: e.tensor_copy(gamo_sb[0:1, 3:4], grow3[0:1, 0:1]), inc=sV)
        dbv = sV.n
        DMA("sync", dbg_rows[0:1, 0:4], gamo_sb[0:1, 0:4], waits=[(sV, dbv), (sO, 16 * n_out)])
        DMA("sync", dbg_rows[0:1, 4:12], th3[0:1, 0:8])
        DMA("sync", dbg_ars[:, :], rs_secs[0][:, :])
        DMA("sync", dbg_hid[:, :], hid[:, :])
        DMA("sync", dbg_w3q[:, :], w3q[:, :])
        for g in range(NGRP):
            DMA("sync", dbg_frs[:, g * 1024:g * 1024 + 1024],
                rs_secs[1 + g][0:1024, 0:1024])
        DMA("sync", dbg_fp[:, 0:1024], part_secs[1][:, 0:1024])
        em("vector", lambda e: e.tensor_copy(fin_o[:, 0:4], thcol[:, :]),
           waits=[(sO, 16 * n_out)])
        em("vector", lambda e: e.tensor_copy(fin_o[:, 4:8], nthcol[:, :]))
        em("vector", lambda e: e.tensor_copy(fin_o[:, 8:9], blcol[:, :]))
        em("vector", lambda e: e.tensor_copy(fin_o[:, 9:10], ombcol[:, :]), inc=sV)
        DMA("sync", dbg_cols[:, :], fin_o[:, 0:16], waits=[(sV, sV.n)])
        em("sync", lambda e: e.nop(), waits=[(sQ, sQ.n)])
    em("sync", lambda e: e.nop(), waits=[(sO, 16 * n_out)])

    with nc.Block() as block:
        bld.flush(block)
    esC.close()
    es0.close()
    return nc


_NC_CACHE = {}


def prep_inputs(inputs):
    x = np.ascontiguousarray(np.asarray(inputs["L0_attn_hidden_states"], np.float32))
    Wq = np.asarray(inputs["L0_attn_Wq"], np.float32)
    Wk = np.asarray(inputs["L0_attn_Wk"], np.float32)
    Wv = np.asarray(inputs["L0_attn_Wv"], np.float32)
    Wo = np.asarray(inputs["L0_attn_Wo"], np.float32)
    spos = np.asarray(inputs["L0_attn_splat_positions"], np.float32)
    sdir = np.asarray(inputs["L0_attn_splat_directions"], np.float32)
    sls = np.asarray(inputs["L0_attn_splat_log_scales"], np.float32)
    sla = np.asarray(inputs["L0_attn_splat_log_amplitudes"], np.float32)
    ds = np.float32(inputs["L0_attn_directional_strength"])
    ps = np.float32(inputs["L0_attn_position_sensitivity"])
    gs = np.float32(inputs["L0_attn_gsa_strength"])
    w1f = np.asarray(inputs["L0_ffn_w1"], np.float32)
    w2f = np.asarray(inputs["L0_ffn_w2"], np.float32)
    w3f = np.asarray(inputs["L0_ffn_w3"], np.float32)

    def blockT(w_rows):
        nb = w_rows.shape[0] // 128
        t = w_rows.reshape(nb, 128, 16, 128)
        return np.ascontiguousarray(t.transpose(0, 3, 2, 1).reshape(nb, 128, 2048))

    scalv = np.array([[ds, ps, gs, 0.0]], np.float32)
    share = {}
    for b in range(4):
        xt = x[b].T
        share[b] = np.ascontiguousarray(
            xt.reshape(16, 128, S // TSTR, TSTR).transpose(1, 2, 0, 3).reshape(128, 16 * S))
    wq_b = blockT(Wq)
    wk_b = blockT(Wk)
    wv_b = blockT(Wv)
    woT = np.ascontiguousarray(Wo.T.reshape(16, 128, 2048))
    w1_b = blockT(w1f)
    w2_b = blockT(w2f)
    w3T = w3f.T

    w3_maps = {}
    for p in range(2):
        my = w3T[p * FH:(p + 1) * FH].reshape(22, 128, 4, 512)
        parts = []
        for (goff, gsz) in ((0, 8), (8, 7), (15, 7)):
            for dc in range(4):
                blk = my[goff:goff + gsz, :, dc, :]
                parts.append(blk.transpose(1, 0, 2).reshape(128, gsz * 512))
        w3_maps[p] = np.ascontiguousarray(np.concatenate(parts, axis=1))
    in_maps = []
    for c in range(8):
        b, p = c // 2, c % 2
        mh = slice(p * 8, (p + 1) * 8)
        mf = slice(p * FC, (p + 1) * FC)
        in_maps.append({
            "xTb": share[b],
            "xOwn": np.ascontiguousarray(x[b, p * 1024:(p + 1) * 1024]),
            "wq": np.ascontiguousarray(wq_b[mh]),
            "wk": np.ascontiguousarray(wk_b[mh]),
            "wv": np.ascontiguousarray(wv_b[mh]),
            "wo": np.ascontiguousarray(woT[mh]),
            "w1": np.ascontiguousarray(w1_b[mf]),
            "w2": np.ascontiguousarray(w2_b[mf]),
            "w3": w3_maps[p],
            "pos": np.ascontiguousarray(spos[mh].transpose(1, 0, 2).reshape(NSP, H * HD)),
            "dirs": np.ascontiguousarray(sdir[mh].transpose(1, 0, 2).reshape(NSP, H * HD)),
            "lsT": np.ascontiguousarray(sls[mh].T),
            "laT": np.ascontiguousarray(sla[mh].T),
            "scal": scalv,
        })

    return in_maps


def kernel(**inputs):
    import os
    in_maps = prep_inputs(inputs)
    safe = int(os.environ.get("KSAFE", "0"))
    key = ("nc", safe)
    if key not in _NC_CACHE:
        _NC_CACHE[key] = build(safe=safe)
    nc = _NC_CACHE[key]
    res = run_bass_kernel_spmd(nc, in_maps, core_ids=list(range(8)))
    outf = np.empty((4, S, DM), np.float32)
    for c in range(8):
        b, p = c // 2, c % 2
        outf[b, p * 1024:(p + 1) * 1024] = np.asarray(res.results[c]["out"], np.float32)
    return outf
